# version 08
# baseline (speedup 1.0000x reference)
import numpy as np
import concourse.bass as bass
import concourse.mybir as mybir
from concourse.bass_utils import run_bass_kernel_spmd
from contextlib import ExitStack

F32 = mybir.dt.float32
BF16 = mybir.dt.bfloat16
AF = mybir.ActivationFunctionType
ALU = mybir.AluOpType
AX = mybir.AxisListType

NCORES = 8
B = 2
N = 16384
C = 1024
TLOC = N // NCORES
TILE = 128
NT = TLOC // TILE
TW = TLOC + TILE
SCALE2 = 1.0 / float(N)
INV_C = 1.0 / float(C)
RS_HALF = float(np.sqrt(0.5))


def _even(ap):
    return ap.rearrange("p (i two) -> p i two", two=2)[:, :, 0]


def _odd(ap):
    return ap.rearrange("p (i two) -> p i two", two=2)[:, :, 1]


def _ck(ap):
    return ap.rearrange("p (ck t) -> p ck t", ck=8)


def build_a(ntb=NT):
    nc = bass.Bass("TRN2", target_bir_lowering=False,
                   detect_race_conditions=False)
    xcm = nc.dram_tensor("xcm", [B, C, TW], F32, kind="ExternalInput").ap()
    tab = nc.dram_tensor("tab", [TW, C], F32, kind="ExternalInput").ap()
    wk = nc.dram_tensor("wk", [C, C], F32, kind="ExternalInput").ap()
    wv = nc.dram_tensor("wv", [C, C], F32, kind="ExternalInput").ap()
    bias3 = nc.dram_tensor("bias3", [3, C], F32, kind="ExternalInput").ap()
    kvp = nc.dram_tensor("kvp", [128, 2 * C], F32, kind="ExternalOutput").ap()
    ksp = nc.dram_tensor("ksp", [1, 2 * C], F32, kind="ExternalOutput").ap()

    ctx = ExitStack()
    sb = lambda nm, shp, dt=F32: ctx.enter_context(nc.sbuf_tensor(nm, shp, dt))
    ps = lambda nm, shp, dt=F32: ctx.enter_context(nc.psum_tensor(nm, shp, dt))
    sem = lambda nm: ctx.enter_context(nc.semaphore(nm))

    wk_sb = sb("wk_sb", [128, 8 * C], BF16)
    wv_sb = sb("wv_sb", [128, 8 * C], BF16)
    bk_sb = sb("bk_sb", [1, C], BF16)
    bv_sb = sb("bv_sb", [1, C], BF16)
    ones1 = sb("ones1", [1, 128], BF16)
    ones_f = sb("ones_f", [128, 1], F32)
    xbf = [sb(f"xbf{p}", [128, 8 * TILE], BF16) for p in range(2)]
    tabt = [sb(f"tabt{p}", [128, C], F32) for p in range(2)]
    k_sb = [sb(f"k_sb{p}", [128, C], F32) for p in range(2)]
    kr_bf = [sb(f"kr_bf{p}", [128, C], BF16) for p in range(2)]
    v_bf = [sb(f"v_bf{p}", [128, C], BF16) for p in range(2)]
    t1 = sb("t1", [128, 512], F32)
    t2 = sb("t2", [128, 512], F32)
    ksacc = [sb(f"ksacc{bb}", [128, C], F32) for bb in range(2)]
    kvs_sb = sb("kvs_sb", [128, 2 * C], F32)
    ksp_sb = sb("ksp_sb", [1, 2 * C], F32)

    PK = [ps("PK0", [128, C]), ps("PK1", [128, C])]
    PV = ps("PV", [128, C])
    PKV = ps("PKV", [128, C])

    s_xcE, s_xcO = sem("s_xcE"), sem("s_xcO")
    s_tabE, s_tabO = sem("s_tabE"), sem("s_tabO")
    s_w, s_gc = sem("s_w"), sem("s_gc")
    s_tk, s_tv, s_t2 = sem("s_tk"), sem("s_tv"), sem("s_t2")
    s_gel, s_vev = sem("s_gel"), sem("s_vev")
    s_v1, s_kve = sem("s_v1"), sem("s_kve")
    s_ksf, s_kse = sem("s_ksf"), sem("s_kse")
    s_out = sem("s_out")

    J = 2 * ntb

    xc_sem = lambda j: s_xcE if j % 2 == 0 else s_xcO
    tab_sem = lambda j: s_tabE if j % 2 == 0 else s_tabO

    with nc.Block() as block:
        @block.gpsimd
        def _(g):
            g.dma_start(out=wk_sb[:].rearrange("p (ck d) -> p ck d", ck=8), in_=wk.rearrange("(ck p) d -> p ck d", p=128)).then_inc(s_w, 16)
            g.dma_start(out=wv_sb[:].rearrange("p (ck d) -> p ck d", ck=8), in_=wv.rearrange("(ck p) d -> p ck d", p=128)).then_inc(s_w, 16)
            g.dma_start(out=bk_sb[:], in_=bias3[1:2, :]).then_inc(s_w, 16)
            g.dma_start(out=bv_sb[:], in_=bias3[2:3, :]).then_inc(s_w, 16)
            g.memset(ones1[:], 1.0)
            g.memset(ones_f[:], 1.0).then_inc(s_gc, 1)
            for j in range(J):
                b, i = divmod(j, ntb)
                if j >= 2:
                    g.wait_ge(s_tv, j - 1)
                row = (i + 1) * TILE
                src = xcm[b, :, row:row + TILE].rearrange(
                    "(ck p) t -> p ck t", p=128)
                g.dma_start(out=xbf[j % 2][:].rearrange("p (ck t) -> p ck t", ck=8),
                            in_=src).then_inc(xc_sem(j), 16)

        @block.sync
        def _(sy):
            for j in range(J):
                b, i = divmod(j, ntb)
                if j >= 2:
                    sy.wait_ge(s_v1, j - 1)
                row = (i + 1) * TILE
                sy.dma_start(out=tabt[j % 2][:], in_=tab[row:row + TILE, :]
                             ).then_inc(tab_sem(j), 16)
            sy.wait_ge(s_kve, 2)
            sy.wait_ge(s_kse, 2)
            sy.dma_start(out=kvp[:], in_=kvs_sb[:]).then_inc(s_out, 16)
            sy.dma_start(out=ksp[:], in_=ksp_sb[:]).then_inc(s_out, 16)
            sy.wait_ge(s_out, 32)

        @block.tensor
        def _(t):
            t.wait_ge(s_w, 64)
            t.wait_ge(s_gc, 1)

            def kv_mms(jj):
                bb, ii = divmod(jj, ntb)
                t.wait_ge(s_v1, jj + 1)
                t.wait_ge(s_vev, jj + 1)
                if ii == 0 and bb == 1:
                    t.wait_ge(s_kve, 1)
                for p in range(8):
                    st = (ii == 0) and (p in (0, 4))
                    sp = (ii == ntb - 1) and (p in (3, 7))
                    mm = t.matmul(PKV[:, p * 128:(p + 1) * 128],
                                  kr_bf[jj % 2][:, p * 128:(p + 1) * 128],
                                  v_bf[jj % 2][:, p * 128:(p + 1) * 128],
                                  start=st, stop=sp)
                mm.then_inc(s_t2, 1)

            for j in range(J):
                t.wait_ge(xc_sem(j), 16 * (j // 2 + 1))
                if j >= 2:
                    t.wait_ge(s_gel, j - 1)
                pk = PK[j % 2]
                xb = xbf[j % 2]
                for ck in range(8):
                    for h in range(2):
                        t.matmul(pk[:, h * 512:(h + 1) * 512],
                                 xb[:, ck * 128:(ck + 1) * 128],
                                 wk_sb[:, ck * C + h * 512: ck * C + (h + 1) * 512],
                                 start=(ck == 0), stop=False)
                for h in range(2):
                    mm = t.matmul(pk[:, h * 512:(h + 1) * 512], ones1[:],
                                  bk_sb[0:1, h * 512:(h + 1) * 512],
                                  start=False, stop=True)
                mm.then_inc(s_tk, 1)
                if j >= 1:
                    t.wait_ge(s_vev, j)
                for ck in range(8):
                    for h in range(2):
                        t.matmul(PV[:, h * 512:(h + 1) * 512],
                                 xb[:, ck * 128:(ck + 1) * 128],
                                 wv_sb[:, ck * C + h * 512: ck * C + (h + 1) * 512],
                                 start=(ck == 0), stop=False)
                for h in range(2):
                    mm = t.matmul(PV[:, h * 512:(h + 1) * 512], ones1[:],
                                  bv_sb[0:1, h * 512:(h + 1) * 512],
                                  start=False, stop=True)
                mm.then_inc(s_tv, 1)
                if j >= 1:
                    kv_mms(j - 1)
            kv_mms(J - 1)
            t.wait_ge(s_kve, 2)
            t.wait_ge(s_v1, J)
            for b in range(2):
                if b == 1:
                    t.wait_ge(s_kse, 1)
                for h in range(2):
                    mm = t.matmul(PKV[0:1, h * 512:(h + 1) * 512], ones_f[:],
                                  ksacc[b][:, h * 512:(h + 1) * 512],
                                  start=True, stop=True)
                mm.then_inc(s_ksf, 1)

        @block.scalar
        def _(s):
            for j in range(J):
                s.wait_ge(s_tk, j + 1)
                if j >= 2:
                    s.wait_ge(s_v1, j - 1)
                s.activation(k_sb[j % 2][:], PK[j % 2][:], AF.Gelu
                             ).then_inc(s_gel, 1)
                s.wait_ge(s_tv, j + 1)
                if j >= 2:
                    s.wait_ge(s_t2, j - 1)
                s.copy(v_bf[j % 2][:], PV[:]).then_inc(s_vev, 1)

        @block.vector
        def _(v):
            for j in range(J):
                b, i = divmod(j, ntb)
                v.wait_ge(s_gel, j + 1)
                v.wait_ge(tab_sem(j), 16 * (j // 2 + 1))
                if j >= 2:
                    v.wait_ge(s_t2, j - 1)
                ks = k_sb[j % 2]
                if i == 0:
                    v.tensor_copy(ksacc[b][:], ks[:])
                else:
                    v.tensor_add(ksacc[b][:], ksacc[b][:], ks[:])
                cosv = tabt[j % 2][:, 0:512]
                sinv = tabt[j % 2][:, 512:1024]
                kr = kr_bf[j % 2]
                v.scalar_tensor_tensor(t1[:], _even(ks[:]), 0.21, cosv,
                                       ALU.add, ALU.mult)
                v.scalar_tensor_tensor(t2[:], _odd(ks[:]), 0.21, sinv,
                                       ALU.add, ALU.mult)
                v.drain()
                v.tensor_sub(_even(kr[:]), t1[:], t2[:])
                v.drain()
                v.scalar_tensor_tensor(t1[:], _odd(ks[:]), 0.21, cosv,
                                       ALU.add, ALU.mult)
                v.scalar_tensor_tensor(t2[:], _even(ks[:]), 0.21, sinv,
                                       ALU.add, ALU.mult)
                v.drain()
                v.tensor_add(_odd(kr[:]), t1[:], t2[:]).then_inc(s_v1, 1)
                if i == ntb - 1:
                    v.wait_ge(s_t2, 16 * (b + 1))
                    v.tensor_scalar(kvs_sb[:, b * C:(b + 1) * C], PKV[:],
                                    SCALE2, None, ALU.mult).then_inc(s_kve, 1)
            for b in range(2):
                v.wait_ge(s_ksf, b + 1)
                v.tensor_copy(ksp_sb[0:1, b * C:(b + 1) * C], PKV[0:1, :]).then_inc(s_kse, 1)

    ctx.close()
    return nc


def build_b(tj=None, debug=False):
    nc = bass.Bass("TRN2", target_bir_lowering=False,
                   detect_race_conditions=False)
    xcm = nc.dram_tensor("xcm", [B, C, TW], F32, kind="ExternalInput").ap()
    xtm = nc.dram_tensor("xtm", [B, TW, C], F32, kind="ExternalInput").ap()
    tab = nc.dram_tensor("tab", [TW, C], F32, kind="ExternalInput").ap()
    wq = nc.dram_tensor("wq", [C, C], F32, kind="ExternalInput").ap()
    wr = nc.dram_tensor("wr", [C, C], F32, kind="ExternalInput").ap()
    wkf = nc.dram_tensor("wkf", [C, C], F32, kind="ExternalInput").ap()
    wvf = nc.dram_tensor("wvf", [C, C], F32, kind="ExternalInput").ap()
    bias3 = nc.dram_tensor("bias3", [3, C], F32, kind="ExternalInput").ap()
    gb = nc.dram_tensor("gb", [2, C], F32, kind="ExternalInput").ap()
    mus = nc.dram_tensor("mus", [2, C], F32, kind="ExternalInput").ap()
    mask = nc.dram_tensor("mask", [128, 1], F32, kind="ExternalInput").ap()
    ident = nc.dram_tensor("ident", [128, 128], F32, kind="ExternalInput").ap()
    kv_all = nc.dram_tensor("kv_all", [NCORES, 128, 2 * C], F32,
                            kind="ExternalInput").ap()
    ks_in = nc.dram_tensor("ks_in", [NCORES, 2 * C], F32,
                           kind="ExternalInput").ap()
    y = nc.dram_tensor("y", [B, TLOC, C], F32, kind="ExternalOutput").ap()
    x1_dr = nc.dram_tensor("x1_dr", [B, TW, C], F32, kind="Internal").ap()
    km_dr = nc.dram_tensor("km_dr", [1, 2 * C], F32, kind="Internal").ap()
    ch_dr = nc.dram_tensor("ch_dr", [1, 32], F32, kind="Internal").ap()

    ctx = ExitStack()
    sb = lambda nm, shp, dt=F32: ctx.enter_context(nc.sbuf_tensor(nm, shp, dt))
    ps = lambda nm, shp, dt=F32: ctx.enter_context(nc.psum_tensor(nm, shp, dt))
    sem = lambda nm: ctx.enter_context(nc.semaphore(nm))

    wq_sb = sb("wq_sb", [128, 8 * C], BF16)
    wr_sb = sb("wr_sb", [128, 8 * C], BF16)
    wkf_sb = sb("wkf_sb", [128, 8 * C], BF16)
    wvf_sb = sb("wvf_sb", [128, 8 * C], BF16)
    bq_sb = sb("bq_sb", [1, C], BF16)
    ones1 = sb("ones1", [1, 128], BF16)
    identb = sb("identb", [128, 128], BF16)
    mask_sb = sb("mask_sb", [128, 1], F32)
    kvacc = sb("kvacc", [128, 2 * C], F32)
    kv_bd = sb("kv_bd", [128, 2 * C], BF16)
    kmb = sb("kmb", [128, 2 * C], F32)
    chb = sb("chb", [128, 32], F32)
    gamb = sb("gamb", [128, C], F32)
    betb = sb("betb", [128, C], F32)
    murb = sb("murb", [128, C], F32)
    mukb = sb("mukb", [128, C], F32)
    eps12 = sb("eps12", [128, 1], F32)
    eps5 = sb("eps5", [128, 1], F32)
    xbf = [sb(f"xbf{p}", [128, 8 * TILE], BF16) for p in range(2)]
    tabt = [sb(f"tabt{p}", [128, C], F32) for p in range(2)]
    xts = [sb(f"xts{p}", [128, C], F32) for p in range(2)]
    q_sb = [sb(f"q_sb{p}", [128, C], F32) for p in range(2)]
    zt_sb = [sb(f"zt_sb{p}", [128, C], F32) for p in range(2)]
    z16 = [sb(f"z16_{p}", [128, 16], F32) for p in range(2)]
    qro = [sb(f"qro{p}", [128, C], F32) for p in range(2)]
    qr_bf = [sb(f"qr_bf{p}", [128, C], BF16) for p in range(2)]
    qrT_sb = [sb(f"qrT_sb{p}", [128, C], BF16) for p in range(2)]
    t1 = sb("t1", [128, 512], F32)
    t2 = sb("t2", [128, 512], F32)
    res1 = [sb(f"res1_{p}", [128, C], F32) for p in range(2)]
    y1_sb = [sb(f"y1_sb{p}", [128, C], F32) for p in range(2)]
    xbg = [sb(f"xbg{p}", [128, C], F32) for p in range(2)]
    tt_sb = [sb(f"tt_sb{p}", [128, C], F32) for p in range(2)]
    res2 = [sb(f"res2_{p}", [128, C], F32) for p in range(2)]
    x1_sb = [sb(f"x1_sb{p}", [128, C], F32) for p in range(2)]
    sumst = sb("sumst", [128, 8], F32)
    sqst = sb("sqst", [128, 8], F32)
    stt_ = sb("stt_", [128, 32], F32)
    dump = q_sb

    x1t = [q_sb[0], q_sb[1]]
    xpt = [zt_sb[0], zt_sb[1]]
    d_sb = [qro[0], qro[1]]
    xr_bf = [qr_bf[0][:], res1[0][:].bitcast(BF16)[:, 0:C]]
    xk_bf = [qrT_sb[0][:], res1[1][:].bitcast(BF16)[:, 0:C]]
    xrT_sb = [y1_sb[0][:].bitcast(BF16)[:, 0:C], y1_sb[1][:].bitcast(BF16)[:, 0:C]]
    xkT_sb = [tt_sb[0][:].bitcast(BF16)[:, 0:C], tt_sb[1][:].bitcast(BF16)[:, 0:C]]
    tr_sb = [res2[0], res2[1]]
    ug_sb = [xbg[0], xbg[1]]
    kact_bf = [x1_sb[0][:].bitcast(BF16)[:, 0:C], x1_sb[1][:].bitcast(BF16)[:, 0:C]]
    kactT_sb = [x1_sb[0][:].bitcast(BF16)[:, C:2 * C], x1_sb[1][:].bitcast(BF16)[:, C:2 * C]]
    fo_sb = [xts[0], xts[1]]
    res3 = [tabt[0], tabt[1]]
    y_sb = [kvacc[:, 0:C], kvacc[:, C:2 * C]]
    dump2 = [qro[0], qro[1]]
    krelu2 = [qro[0], qro[1]]

    PQ = ps("PQ", [128, C])
    PTQ = [ps("PTQ0", [128, C], BF16), ps("PTQ1", [128, C], BF16)]
    PO = [ps("PO0", [128, C]), ps("PO1", [128, C])]

    names = ["sVZ", "sTQ", "sSG", "sTT", "sEQ", "sTO", "sR1", "sS1", "sVa1",
             "sSb1", "sVc1", "sS2", "sR2", "sS3", "sVa2", "sSb2", "sVc2",
             "sS4", "sGX",
             "sGD", "sGM", "sVM", "sTX2", "sEX", "sTR", "sSR", "sTK", "sSKr",
             "sSK", "sTKT", "sEK", "sTV", "sVG", "sS5", "sVa3", "sSb3",
             "sVc3", "sS6"]
    S = {nm: sem(nm) for nm in names}
    s_xcE, s_xcO = sem("s_xcE"), sem("s_xcO")
    s_tabE, s_tabO = sem("s_tabE"), sem("s_tabO")
    s_xtE, s_xtO = sem("s_xtE"), sem("s_xtO")
    s_x1E, s_x1O = sem("s_x1E"), sem("s_x1O")
    s_xpE, s_xpO = sem("s_xpE"), sem("s_xpO")
    s_w = sem("s_w")
    s_w2 = sem("s_w2")
    s_acc = sem("s_acc")
    s_ack = sem("s_ack")
    s_set = sem("s_set")
    s_kmb = sem("s_kmb")
    s_gc = sem("s_gc")
    s_y = sem("s_y")
    s_xs = sem("s_xs")
    s_pb = sem("s_pb")

    def stc(par, ln, k):
        return stt_[:, par * 16 + ln * 5 + k: par * 16 + ln * 5 + k + 1]

    def sucol(par, ln):
        return sumst[:, par * 4 + ln: par * 4 + ln + 1]

    def sqcol(par, ln):
        return sqst[:, par * 4 + ln: par * 4 + ln + 1]

    xc_sem = lambda j: s_xcE if j % 2 == 0 else s_xcO
    tab_sem = lambda j: s_tabE if j % 2 == 0 else s_tabO
    xt_sem = lambda j: s_xtE if j % 2 == 0 else s_xtO
    x1_sem = lambda j: s_x1E if j % 2 == 0 else s_x1O
    xp_sem = lambda j: s_xpE if j % 2 == 0 else s_xpO

    tiles = [(b, i) for i in range(NT + 1) for b in range(2)]
    if tj is not None:
        tiles = tiles[:tj]
    J = len(tiles)
    tiles2 = [(b, i) for i in range(1, NT + 1) for b in range(2)]
    if tj is not None:
        tiles2 = []
    J2 = len(tiles2)

    with nc.Block() as block:
        @block.gpsimd
        def _(g):
            g.dma_start(out=wq_sb[:].rearrange("p (ck d) -> p ck d", ck=8), in_=wq.rearrange("(ck p) d -> p ck d", p=128)).then_inc(s_w, 16)
            g.dma_start(out=wr_sb[:].rearrange("p (ck d) -> p ck d", ck=8), in_=wr.rearrange("(ck p) d -> p ck d", p=128)).then_inc(s_w, 16)
            g.dma_start(out=wkf_sb[:].rearrange("p (ck d) -> p ck d", ck=8), in_=wkf.rearrange("(ck p) d -> p ck d", p=128)).then_inc(s_w, 16)
            g.dma_start(out=wvf_sb[:].rearrange("p (ck d) -> p ck d", ck=8), in_=wvf.rearrange("(ck p) d -> p ck d", p=128)).then_inc(s_w, 16)
            g.dma_start(out=bq_sb[:], in_=bias3[0:1, :]).then_inc(s_w, 16)
            g.dma_start(out=identb[:], in_=ident[:]).then_inc(s_w, 16)
            g.dma_start(out=gamb[:], in_=gb[0:1, :].broadcast_to([128, C])).then_inc(s_w, 16)
            g.dma_start(out=betb[:], in_=gb[1:2, :].broadcast_to([128, C])).then_inc(s_w, 16)
            g.dma_start(out=murb[:], in_=mus[0:1, :].broadcast_to([128, C])).then_inc(s_w, 16)
            g.dma_start(out=mukb[:], in_=mus[1:2, :].broadcast_to([128, C])).then_inc(s_w, 16)
            g.memset(ones1[:], 1.0)
            g.memset(eps12[:], 1e-12)
            g.memset(eps5[:], 1e-5).then_inc(s_gc, 1)
            for j in (0, 1)[:len(tiles)]:
                b, i = tiles[j]
                src = xcm[b, :, i * TILE:(i + 1) * TILE].rearrange(
                    "(ck p) t -> p ck t", p=128)
                g.dma_start(out=xbf[j % 2][:].rearrange("p (ck t) -> p ck t", ck=8),
                            in_=src).then_inc(xc_sem(j), 16)
            g.dma_start(out=kvacc[:], in_=kv_all[0]).then_inc(s_acc, 16)
            for k in range(1, NCORES):
                g.wait_ge(s_acc, 16 * k)
                g.dma_start(out=kvacc[:], in_=kv_all[k], accum_op=ALU.add
                            ).then_inc(s_acc, 16)
            g.dma_start(out=kmb[0:1, :], in_=ks_in[0:1, :]).then_inc(s_ack, 16)
            for k in range(1, NCORES):
                g.wait_ge(s_ack, 16 * k)
                g.dma_start(out=kmb[0:1, :], in_=ks_in[k:k + 1, :],
                            accum_op=ALU.add).then_inc(s_ack, 16)
            g.wait_ge(s_set, 1)
            g.dma_start(out=km_dr[:], in_=kmb[0:1, :]).then_inc(s_kmb, 16)
            g.dma_start(out=ch_dr[:], in_=chb[0:1, 0:32]).then_inc(s_kmb, 16)
            g.wait_ge(s_kmb, 32)
            g.dma_start(out=kmb[:, 0:C], in_=km_dr[0:1, 0:C].broadcast_to([128, C])).then_inc(s_kmb, 16)
            g.dma_start(out=kmb[:, C:2 * C], in_=km_dr[0:1, C:2 * C].broadcast_to([128, C])).then_inc(s_kmb, 16)
            g.dma_start(out=chb[:, 0:16], in_=ch_dr[0:1, 0:16].broadcast_to([128, 16])).then_inc(s_kmb, 16)
            g.dma_start(out=chb[:, 16:32], in_=ch_dr[0:1, 16:32].broadcast_to([128, 16])).then_inc(s_kmb, 16)
            for j, (b, i) in enumerate(tiles):
                if j >= 2:
                    g.wait_ge(S["sTQ"], j - 1)
                    src = xcm[b, :, i * TILE:(i + 1) * TILE].rearrange(
                        "(ck p) t -> p ck t", p=128)
                    g.dma_start(out=xbf[j % 2][:].rearrange("p (ck t) -> p ck t", ck=8),
                                in_=src).then_inc(xc_sem(j), 16)
                g.wait_ge(xt_sem(j), 16 * (j // 2 + 1))
                if j >= 2:
                    g.wait_ge(S["sR2"], j - 1)
                g.tensor_add(xbg[j % 2][:], xts[j % 2][:], betb[:]
                             ).then_inc(S["sGX"], 1)
            g.nop().then_inc(s_pb, 1)
            g.wait_ge(s_pb, 5)
            for j, (b, i) in enumerate(tiles2):
                g.wait_ge(x1_sem(j), 16 * (j // 2 + 1))
                g.wait_ge(xp_sem(j), 16 * (j // 2 + 1))
                if j >= 2:
                    g.wait_ge(S["sVM"], j - 1)
                    g.wait_ge(S["sS5"], j - 1)
                g.tensor_sub(d_sb[j % 2][:], x1t[j % 2][:], xpt[j % 2][:]
                             ).then_inc(S["sGD"], 1)
                if j >= 2:
                    g.wait_ge(S["sTK"], j - 1)
                g.tensor_mul(ug_sb[j % 2][:], d_sb[j % 2][:], mukb[:])
                g.drain()
                g.tensor_add(xk_bf[j % 2], xpt[j % 2][:], ug_sb[j % 2][:]
                             ).then_inc(S["sGM"], 1)

        @block.sync
        def _(sy):
            sy.dma_start(out=mask_sb[:], in_=mask[:]).then_inc(s_w2, 16)
            for j, (b, i) in enumerate(tiles):
                row = i * TILE
                if j >= 2:
                    sy.wait_ge(S["sVZ"], j - 1)
                sy.dma_start(out=tabt[j % 2][:], in_=tab[row:row + TILE, :]
                             ).then_inc(tab_sem(j), 16)
                if j >= 2:
                    sy.wait_ge(S["sR2"], j - 1)
                    sy.wait_ge(S["sGX"], j - 1)
                sy.dma_start(out=xts[j % 2][:], in_=xtm[b, row:row + TILE, :]
                             ).then_inc(xt_sem(j), 16)
                sy.wait_ge(S["sS4"], j + 1)
                sy.dma_start(out=x1_dr[b, row:row + TILE, :],
                             in_=x1_sb[j % 2][:]).then_inc(s_xs, 16)
            sy.nop().then_inc(s_pb, 1)
            sy.wait_ge(s_pb, 5)
            sy.wait_ge(s_xs, 16 * J)
            for j, (b, i) in enumerate(tiles2):
                row = i * TILE
                if j >= 2:
                    sy.wait_ge(S["sVG"], j - 1)
                sy.dma_start(out=x1t[j % 2][:], in_=x1_dr[b, row:row + TILE, :]
                             ).then_inc(x1_sem(j), 16)
                if j >= 2:
                    sy.wait_ge(S["sGM"], j - 1)
                    sy.wait_ge(S["sVM"], j - 1)
                sy.dma_start(out=xpt[j % 2][:],
                             in_=x1_dr[b, row - 1:row + TILE - 1, :]
                             ).then_inc(xp_sem(j), 16)
                sy.wait_ge(S["sS6"], j + 1)
                sy.dma_start(out=y[b, (i - 1) * TILE:i * TILE, :],
                             in_=y_sb[j % 2]).then_inc(s_y, 16)
            if J2 > 0:
                sy.wait_ge(s_y, 16 * J2)

        @block.tensor
        def _(t):
            t.wait_ge(s_w, 16 * 10)
            t.wait_ge(s_gc, 1)
            for j, (b, i) in enumerate(tiles):
                t.wait_ge(xc_sem(j), 16 * (j // 2 + 1))
                if j >= 1:
                    t.wait_ge(S["sSG"], j)
                xbfj = xbf[j % 2]
                for ck in range(8):
                    for h in range(2):
                        t.matmul(PQ[:, h * 512:(h + 1) * 512],
                                 xbfj[:, ck * 128:(ck + 1) * 128],
                                 wq_sb[:, ck * C + h * 512: ck * C + (h + 1) * 512],
                                 start=(ck == 0), stop=False)
                for h in range(2):
                    mm = t.matmul(PQ[:, h * 512:(h + 1) * 512], ones1[:],
                                  bq_sb[0:1, h * 512:(h + 1) * 512],
                                  start=False, stop=True)
                mm.then_inc(S["sTQ"], 1)
                t.wait_ge(S["sVZ"], j + 1)
                if j >= 2:
                    t.wait_ge(S["sEQ"], j - 1)
                for ck in range(8):
                    mm = t.transpose(PTQ[j % 2][:, ck * 128:(ck + 1) * 128],
                                     qr_bf[j % 2][:, ck * 128:(ck + 1) * 128],
                                     identb[:])
                mm.then_inc(S["sTT"], 1)
                t.wait_ge(S["sEQ"], j + 1)
                if j >= 2:
                    t.wait_ge(S["sR1"], j - 1)
                if j == 0:
                    t.wait_ge(s_set, 2)
                for p in range(8):
                    mm = t.matmul(PO[j % 2][:, p * 128:(p + 1) * 128],
                                  qrT_sb[j % 2][:, p * 128:(p + 1) * 128],
                                  kv_bd[:, b * C + p * 128: b * C + (p + 1) * 128],
                                  start=(p in (0, 4)), stop=(p in (3, 7)))
                mm.then_inc(S["sTO"], 1)
            t.transpose(PTQ[0][:, 0:128], identb[:], identb[:]).then_inc(s_pb, 1)
            t.wait_ge(s_pb, 5)
            for j, (b, i) in enumerate(tiles2):
                t.wait_ge(S["sVM"], j + 1)
                if j >= 1:
                    t.wait_ge(S["sEX"], j)
                for ck in range(8):
                    t.transpose(PTQ[j % 2][:, ck * 128:(ck + 1) * 128],
                                xr_bf[j % 2][:, ck * 128:(ck + 1) * 128],
                                identb[:])
                t.wait_ge(S["sGM"], j + 1)
                if j >= 1:
                    t.wait_ge(S["sEK"], j)
                for ck in range(8):
                    mm = t.transpose(PTQ[(j + 1) % 2][:, ck * 128:(ck + 1) * 128],
                                     xk_bf[j % 2][:, ck * 128:(ck + 1) * 128],
                                     identb[:])
                mm.then_inc(S["sTX2"], 1)
                t.wait_ge(S["sEX"], j + 1)
                if j >= 1:
                    t.wait_ge(S["sSR"], j)
                for ck in range(8):
                    for h in range(2):
                        mm = t.matmul(PQ[:, h * 512:(h + 1) * 512],
                                      xrT_sb[j % 2][:, ck * 128:(ck + 1) * 128],
                                      wr_sb[:, ck * C + h * 512: ck * C + (h + 1) * 512],
                                      start=(ck == 0), stop=(ck == 7))
                mm.then_inc(S["sTR"], 1)
                if j >= 1:
                    t.wait_ge(S["sSKr"], j)
                for ck in range(8):
                    for h in range(2):
                        mm = t.matmul(PO[0][:, h * 512:(h + 1) * 512],
                                      xkT_sb[j % 2][:, ck * 128:(ck + 1) * 128],
                                      wkf_sb[:, ck * C + h * 512: ck * C + (h + 1) * 512],
                                      start=(ck == 0), stop=(ck == 7))
                mm.then_inc(S["sTK"], 1)
                t.wait_ge(S["sSK"], j + 1)
                t.wait_ge(S["sEX"], j + 1)
                for ck in range(8):
                    mm = t.transpose(PTQ[j % 2][:, ck * 128:(ck + 1) * 128],
                                     kact_bf[j % 2][:, ck * 128:(ck + 1) * 128],
                                     identb[:])
                mm.then_inc(S["sTKT"], 1)
                t.wait_ge(S["sEK"], j + 1)
                if j >= 1:
                    t.wait_ge(S["sVG"], j)
                for ck in range(8):
                    for h in range(2):
                        mm = t.matmul(PO[1][:, h * 512:(h + 1) * 512],
                                      kactT_sb[j % 2][:, ck * 128:(ck + 1) * 128],
                                      wvf_sb[:, ck * C + h * 512: ck * C + (h + 1) * 512],
                                      start=(ck == 0), stop=(ck == 7))
                mm.then_inc(S["sTV"], 1)

        @block.scalar
        def _(s):
            s.wait_ge(s_gc, 1)
            for j, (b, i) in enumerate(tiles):
                real = i > 0
                s.wait_ge(S["sTQ"], j + 1)
                if j >= 2:
                    s.wait_ge(S["sVZ"], j - 1)
                    s.wait_ge(S["sS3"], j - 1)
                s.activation(q_sb[j % 2][:], PQ[:], AF.Gelu).then_inc(S["sSG"], 1)
                s.wait_ge(S["sTT"], j + 1)
                if j >= 2:
                    s.wait_ge(S["sTO"], j - 1)
                s.copy(qrT_sb[j % 2][:], PTQ[j % 2][:]).then_inc(S["sEQ"], 1)
                s.wait_ge(S["sR1"], j + 1)
                s.activation(dump[j % 2][:], res1[j % 2][:], AF.Square,
                             accum_out=sqcol(j % 2, 0))
                s.activation(stc(j % 2, 0, 0), sucol(j % 2, 0), AF.Square,
                             scale=INV_C).then_inc(S["sS1"], 1)
                s.wait_ge(S["sVa1"], j + 1)
                s.activation(stc(j % 2, 0, 2), stc(j % 2, 0, 1), AF.Sqrt,
                             bias=eps12[:, 0:1]).then_inc(S["sSb1"], 1)
                s.wait_ge(S["sVc1"], j + 1)
                if j >= 2:
                    s.wait_ge(S["sR2"], j - 1)
                s.activation(y1_sb[j % 2][:], res1[j % 2][:], AF.Identity,
                             bias=stc(j % 2, 0, 4), scale=stc(j % 2, 0, 3)
                             ).then_inc(S["sS2"], 1)
                s.wait_ge(S["sR2"], j + 1)
                s.activation(dump[j % 2][:], res2[j % 2][:], AF.Square,
                             accum_out=sqcol(j % 2, 1))
                s.activation(stc(j % 2, 1, 0), sucol(j % 2, 1), AF.Square,
                             scale=INV_C).then_inc(S["sS3"], 1)
                s.wait_ge(S["sVa2"], j + 1)
                s.activation(stc(j % 2, 1, 2), stc(j % 2, 1, 1), AF.Sqrt,
                             bias=eps5[:, 0:1]).then_inc(S["sSb2"], 1)
                s.wait_ge(S["sVc2"], j + 1)
                if j >= 2:
                    s.wait_ge(s_xs, 16 * (j - 1))
                if real:
                    s.activation(x1_sb[j % 2][:], res2[j % 2][:], AF.Identity,
                                 bias=stc(j % 2, 1, 4), scale=stc(j % 2, 1, 3)
                                 ).then_inc(S["sS4"], 1)
                else:
                    s.activation(x1_sb[j % 2][:], res2[j % 2][:], AF.Identity,
                                 bias=stc(j % 2, 1, 4), scale=stc(j % 2, 1, 3))
                    s.drain()
                    s.activation(x1_sb[j % 2][:], x1_sb[j % 2][:], AF.Identity,
                                 scale=mask_sb[:, 0:1]).then_inc(S["sS4"], 1)
            s.nop().then_inc(s_pb, 1)
            s.wait_ge(s_pb, 5)
            for j, (b, i) in enumerate(tiles2):
                s.wait_ge(S["sTX2"], j + 1)
                if j >= 2:
                    s.wait_ge(S["sTR"], j - 1)
                    s.wait_ge(S["sTK"], j - 1)
                s.copy(xrT_sb[j % 2], PTQ[j % 2][:])
                s.copy(xkT_sb[j % 2], PTQ[(j + 1) % 2][:]).then_inc(S["sEX"], 1)
                s.wait_ge(S["sTR"], j + 1)
                if j >= 2:
                    s.wait_ge(S["sVG"], j - 1)
                s.activation(tr_sb[j % 2][:], PQ[:], AF.Tanh, scale=0.5
                             ).then_inc(S["sSR"], 1)
                s.wait_ge(S["sTK"], j + 1)
                s.activation(krelu2[j % 2][:], PO[0][:], AF.Relu
                             ).then_inc(S["sSKr"], 1)
                s.drain()
                if j >= 2:
                    s.wait_ge(S["sTKT"], j - 1)
                s.activation(kact_bf[j % 2], krelu2[j % 2][:], AF.Square,
                             scale=RS_HALF).then_inc(S["sSK"], 1)
                s.wait_ge(S["sTKT"], j + 1)
                if j >= 2:
                    s.wait_ge(S["sTV"], j - 1)
                s.copy(kactT_sb[j % 2], PTQ[j % 2][:]).then_inc(S["sEK"], 1)
                s.wait_ge(S["sVG"], j + 1)
                s.activation(dump2[j % 2][:], res3[j % 2][:], AF.Square,
                             accum_out=sqcol(j % 2, 2))
                s.activation(stc(j % 2, 2, 0), sucol(j % 2, 2), AF.Square,
                             scale=INV_C).then_inc(S["sS5"], 1)
                s.wait_ge(S["sVa3"], j + 1)
                s.activation(stc(j % 2, 2, 2), stc(j % 2, 2, 1), AF.Sqrt,
                             bias=eps5[:, 0:1]).then_inc(S["sSb3"], 1)
                s.wait_ge(S["sVc3"], j + 1)
                if j >= 2:
                    s.wait_ge(s_y, 16 * (j - 1))
                s.activation(y_sb[j % 2], res3[j % 2][:], AF.Identity,
                             bias=stc(j % 2, 2, 4), scale=stc(j % 2, 2, 3)
                             ).then_inc(S["sS6"], 1)

        @block.vector
        def _(v):
            def vsync():
                v.drain()

            v.wait_ge(s_acc, 16 * NCORES)
            v.wait_ge(s_ack, 16 * NCORES)
            v.tensor_scalar(kmb[0:1, :], kmb[0:1, :], 1.0 / float(N), 0.21,
                            ALU.mult, ALU.add)
            vsync()
            v.reduce_sum(chb[0:1, 0:32],
                         kmb[0:1, :].rearrange("p (h e) -> p h e", h=32),
                         axis=AX.X)
            vsync()
            v.tensor_scalar(chb[0:1, 0:32], chb[0:1, 0:32], 0.21, None,
                            ALU.mult).then_inc(s_set, 1)
            v.memset(kv_bd[:], 0.0)
            vsync()
            for bb in range(2):
                for p in range(8):
                    off = bb * C + p * 128
                    v.tensor_copy(kv_bd[0:64, off:off + 64],
                                  kvacc[0:64, off:off + 64])
                    v.tensor_copy(kv_bd[64:128, off + 64:off + 128],
                                  kvacc[64:128, off + 64:off + 128])
            v.nop().then_inc(s_set, 1)
            v.wait_ge(s_kmb, 16 * 6)
            v.tensor_scalar(chb[:], chb[:], 1e-6, None, ALU.add
                            ).then_inc(s_set, 1)
            for j, (b, i) in enumerate(tiles):
                v.wait_ge(S["sSG"], j + 1)
                v.wait_ge(tab_sem(j), 16 * (j // 2 + 1))
                if j == 0:
                    v.wait_ge(s_set, 3)
                if j >= 2:
                    v.wait_ge(S["sVZ"], j - 1)
                v.tensor_mul(zt_sb[j % 2][:], q_sb[j % 2][:],
                             kmb[:, b * C:(b + 1) * C])
                v.reduce_sum(z16[j % 2][:],
                             zt_sb[j % 2][:].rearrange("p (h e) -> p h e", h=16),
                             axis=AX.X)
                vsync()
                v.tensor_add(z16[j % 2][:], z16[j % 2][:],
                             chb[:, b * 16:(b + 1) * 16])
                vsync()
                v.reciprocal(z16[j % 2][:], z16[j % 2][:])
                vsync()
                cosv = tabt[j % 2][:, 0:512]
                sinv = tabt[j % 2][:, 512:1024]
                qq = q_sb[j % 2]
                qo = qro[j % 2]
                v.scalar_tensor_tensor(t1[:], _even(qq[:]), 0.21, cosv,
                                       ALU.add, ALU.mult)
                v.scalar_tensor_tensor(t2[:], _odd(qq[:]), 0.21, sinv,
                                       ALU.add, ALU.mult)
                v.tensor_sub(_even(qo[:]), t1[:], t2[:])
                v.scalar_tensor_tensor(t1[:], _odd(qq[:]), 0.21, cosv,
                                       ALU.add, ALU.mult)
                v.scalar_tensor_tensor(t2[:], _even(qq[:]), 0.21, sinv,
                                       ALU.add, ALU.mult)
                v.tensor_add(_odd(qo[:]), t1[:], t2[:])
                vsync()
                if j >= 2:
                    v.wait_ge(S["sTT"], j - 1)
                zview = z16[j % 2][:].unsqueeze(2).broadcast_to([128, 16, 64])
                v.tensor_tensor(qr_bf[j % 2][:].rearrange("p (h e) -> p h e", h=16),
                                qo[:].rearrange("p (h e) -> p h e", h=16),
                                zview, ALU.mult).then_inc(S["sVZ"], 1)
                v.wait_ge(S["sTO"], j + 1)
                v.wait_ge(xt_sem(j), 16 * (j // 2 + 1))
                if j >= 2:
                    v.wait_ge(S["sS2"], j - 1)
                v.scalar_tensor_tensor(res1[j % 2][:], PO[j % 2][:], 1.0,
                                       xts[j % 2][:], ALU.mult, ALU.add,
                                       accum_out=sucol(j % 2, 0)).then_inc(S["sR1"], 1)
                v.wait_ge(S["sS1"], j + 1)
                v.tensor_scalar(stc(j % 2, 0, 1), sqcol(j % 2, 0), INV_C,
                                stc(j % 2, 0, 0), ALU.mult, ALU.subtract
                                ).then_inc(S["sVa1"], 1)
                v.wait_ge(S["sSb1"], j + 1)
                v.reciprocal(stc(j % 2, 0, 3), stc(j % 2, 0, 2))
                vsync()
                v.tensor_scalar(stc(j % 2, 0, 4), sucol(j % 2, 0),
                                stc(j % 2, 0, 3), -INV_C, ALU.mult, ALU.mult
                                ).then_inc(S["sVc1"], 1)
                v.wait_ge(S["sS2"], j + 1)
                v.wait_ge(S["sGX"], j + 1)
                if j >= 2:
                    v.wait_ge(S["sS4"], j - 1)
                v.tensor_mul(tt_sb[j % 2][:], y1_sb[j % 2][:], gamb[:])
                v.scalar_tensor_tensor(res2[j % 2][:], tt_sb[j % 2][:], 1.0,
                                       xbg[j % 2][:], ALU.mult, ALU.add,
                                       accum_out=sucol(j % 2, 1)).then_inc(S["sR2"], 1)
                v.wait_ge(S["sS3"], j + 1)
                v.tensor_scalar(stc(j % 2, 1, 1), sqcol(j % 2, 1), INV_C,
                                stc(j % 2, 1, 0), ALU.mult, ALU.subtract
                                ).then_inc(S["sVa2"], 1)
                v.wait_ge(S["sSb2"], j + 1)
                v.reciprocal(stc(j % 2, 1, 3), stc(j % 2, 1, 2))
                vsync()
                v.tensor_scalar(stc(j % 2, 1, 4), sucol(j % 2, 1),
                                stc(j % 2, 1, 3), -INV_C, ALU.mult, ALU.mult
                                ).then_inc(S["sVc2"], 1)
            v.nop().then_inc(s_pb, 1)
            v.wait_ge(s_pb, 5)
            for j, (b, i) in enumerate(tiles2):
                v.wait_ge(S["sGM"], j + 1)
                if j >= 2:
                    v.wait_ge(S["sTX2"], j - 1)
                v.tensor_mul(d_sb[j % 2][:], d_sb[j % 2][:], murb[:])
                vsync()
                v.tensor_add(xr_bf[j % 2], xpt[j % 2][:], d_sb[j % 2][:]
                             ).then_inc(S["sVM"], 1)
                v.wait_ge(S["sTV"], j + 1)
                v.wait_ge(S["sSR"], j + 1)
                if j >= 2:
                    v.wait_ge(S["sS6"], j - 1)
                v.scalar_tensor_tensor(fo_sb[j % 2][:], tr_sb[j % 2][:], 1.0,
                                       PO[1][:], ALU.add, ALU.mult)
                vsync()
                v.scalar_tensor_tensor(res3[j % 2][:], fo_sb[j % 2][:], 1.0,
                                       x1t[j % 2][:], ALU.mult, ALU.add,
                                       accum_out=sucol(j % 2, 2)).then_inc(S["sVG"], 1)
                v.wait_ge(S["sS5"], j + 1)
                v.tensor_scalar(stc(j % 2, 2, 1), sqcol(j % 2, 2), INV_C,
                                stc(j % 2, 2, 0), ALU.mult, ALU.subtract
                                ).then_inc(S["sVa3"], 1)
                v.wait_ge(S["sSb3"], j + 1)
                v.reciprocal(stc(j % 2, 2, 3), stc(j % 2, 2, 2))
                vsync()
                v.tensor_scalar(stc(j % 2, 2, 4), sucol(j % 2, 2),
                                stc(j % 2, 2, 3), -INV_C, ALU.mult, ALU.mult
                                ).then_inc(S["sVc3"], 1)

    ctx.close()
    return nc


def kernel(L0_attn_x, L0_attn_Wq, L0_attn_bq, L0_attn_Wk, L0_attn_bk,
           L0_attn_Wv, L0_attn_bv, L0_attn_gamma, L0_attn_beta,
           L0_ffn_W_r, L0_ffn_W_k, L0_ffn_W_v, L0_ffn_mu_r, L0_ffn_mu_k):
    x = np.ascontiguousarray(np.asarray(L0_attn_x, dtype=np.float32))
    wqT = np.ascontiguousarray(np.asarray(L0_attn_Wq, np.float32).T)
    wkT = np.ascontiguousarray(np.asarray(L0_attn_Wk, np.float32).T)
    wvT = np.ascontiguousarray(np.asarray(L0_attn_Wv, np.float32).T)
    wrT = np.ascontiguousarray(np.asarray(L0_ffn_W_r, np.float32).T)
    wkfT = np.ascontiguousarray(np.asarray(L0_ffn_W_k, np.float32).T)
    wvfT = np.ascontiguousarray(np.asarray(L0_ffn_W_v, np.float32).T)
    bias3 = np.ascontiguousarray(np.stack([
        np.asarray(L0_attn_bq, np.float32),
        np.asarray(L0_attn_bk, np.float32),
        np.asarray(L0_attn_bv, np.float32)]))
    gbv = np.ascontiguousarray(np.stack([
        np.asarray(L0_attn_gamma, np.float32),
        np.asarray(L0_attn_beta, np.float32)]))
    mus = np.ascontiguousarray(np.stack([
        np.asarray(L0_ffn_mu_r, np.float32),
        np.asarray(L0_ffn_mu_k, np.float32)]))
    ident = np.eye(128, dtype=np.float32)

    import jax
    import jax.numpy as jnp
    with jax.default_device(jax.devices("cpu")[0]):
        ivj = 1.0 / jnp.float32(10000.0) ** (
            jnp.arange(0, C, 2, dtype=jnp.float32) / C)
        angj = jnp.arange(N, dtype=jnp.float32)[:, None] * ivj[None, :]
        tab_full = np.ascontiguousarray(np.concatenate(
            [np.asarray(jnp.cos(angj)), np.asarray(jnp.sin(angj))],
            axis=1).astype(np.float32))

    xcm, xtm, tabs, masks = [], [], [], []
    for c in range(NCORES):
        s = c * TLOC
        if c == 0:
            xs = np.concatenate([x[:, 0:TILE, :], x[:, 0:TLOC, :]], axis=1)
            tb = np.concatenate([tab_full[0:TILE], tab_full[0:TLOC]], axis=0)
        else:
            xs = x[:, s - TILE:s + TLOC, :]
            tb = tab_full[s - TILE:s + TLOC]
        xtm.append(np.ascontiguousarray(xs))
        xcm.append(np.ascontiguousarray(xs.transpose(0, 2, 1)))
        tabs.append(np.ascontiguousarray(tb))
        masks.append(np.full((128, 1), 0.0 if c == 0 else 1.0, np.float32))

    nc_a = build_a()
    in_a = [{"xcm": xcm[c], "tab": tabs[c], "wk": wkT, "wv": wvT,
             "bias3": bias3} for c in range(NCORES)]
    res_a = run_bass_kernel_spmd(nc_a, in_a, core_ids=list(range(NCORES)))

    kv_all = np.ascontiguousarray(
        np.stack([res_a.results[c]["kvp"] for c in range(NCORES)]))
    ks_all = np.ascontiguousarray(
        np.stack([res_a.results[c]["ksp"][0] for c in range(NCORES)]))

    nc_b = build_b()
    in_b = [{"xcm": xcm[c], "xtm": xtm[c], "tab": tabs[c], "wq": wqT,
             "wr": wrT, "wkf": wkfT, "wvf": wvfT, "bias3": bias3, "gb": gbv,
             "mus": mus, "mask": masks[c], "ident": ident,
             "kv_all": kv_all, "ks_in": ks_all} for c in range(NCORES)]
    res_b = run_bass_kernel_spmd(nc_b, in_b, core_ids=list(range(NCORES)))

    out = np.concatenate([res_b.results[c]["y"] for c in range(NCORES)],
                         axis=1)
    return out


# version 11 (final)
# speedup vs baseline: 2.0770x; 2.0770x over previous
import numpy as np
import concourse.bass as bass
import concourse.mybir as mybir
from concourse.bass_utils import run_bass_kernel_spmd
from contextlib import ExitStack

F32 = mybir.dt.float32
BF16 = mybir.dt.bfloat16
AF = mybir.ActivationFunctionType
ALU = mybir.AluOpType
AX = mybir.AxisListType

NCORES = 8
B = 2
N = 16384
C = 1024
TLOC = N // NCORES
TILE = 128
NT = TLOC // TILE
TW = TLOC + TILE
SCALE2 = 1.0 / float(N)
INV_C = 1.0 / float(C)
RS_HALF = float(np.sqrt(0.5))


def _even(ap):
    return ap.rearrange("p (i two) -> p i two", two=2)[:, :, 0]


def _odd(ap):
    return ap.rearrange("p (i two) -> p i two", two=2)[:, :, 1]


def _ck(ap):
    return ap.rearrange("p (ck t) -> p ck t", ck=8)


def build_a(ntb=NT):
    nc = bass.Bass("TRN2", target_bir_lowering=False,
                   detect_race_conditions=False)
    xcm = nc.dram_tensor("xcm", [B, C, TW], F32, kind="ExternalInput").ap()
    tab = nc.dram_tensor("tab", [TW, C], F32, kind="ExternalInput").ap()
    wk = nc.dram_tensor("wk", [C, C], F32, kind="ExternalInput").ap()
    wv = nc.dram_tensor("wv", [C, C], F32, kind="ExternalInput").ap()
    bias3 = nc.dram_tensor("bias3", [3, C], F32, kind="ExternalInput").ap()
    kvp = nc.dram_tensor("kvp", [128, 2 * C], F32, kind="ExternalOutput").ap()
    ksp = nc.dram_tensor("ksp", [1, 2 * C], F32, kind="ExternalOutput").ap()

    ctx = ExitStack()
    sb = lambda nm, shp, dt=F32: ctx.enter_context(nc.sbuf_tensor(nm, shp, dt))
    ps = lambda nm, shp, dt=F32: ctx.enter_context(nc.psum_tensor(nm, shp, dt))
    sem = lambda nm: ctx.enter_context(nc.semaphore(nm))

    wk_sb = sb("wk_sb", [128, 8 * C], BF16)
    wv_sb = sb("wv_sb", [128, 8 * C], BF16)
    bk_sb = sb("bk_sb", [1, C], BF16)
    bv_sb = sb("bv_sb", [1, C], BF16)
    ones1 = sb("ones1", [1, 128], BF16)
    ones_f = sb("ones_f", [128, 1], F32)
    xbf = [sb(f"xbf{p}", [128, 8 * TILE], BF16) for p in range(2)]
    tabt = [sb(f"tabt{p}", [128, C], F32) for p in range(2)]
    k_sb = [sb(f"k_sb{p}", [128, C], F32) for p in range(2)]
    kr_bf = [sb(f"kr_bf{p}", [128, C], BF16) for p in range(2)]
    v_bf = [sb(f"v_bf{p}", [128, C], BF16) for p in range(2)]
    t1 = sb("t1", [128, 512], F32)
    t2 = sb("t2", [128, 512], F32)
    ksacc = [sb(f"ksacc{bb}", [128, C], F32) for bb in range(2)]
    kvs_sb = sb("kvs_sb", [128, 2 * C], F32)
    ksp_sb = sb("ksp_sb", [1, 2 * C], F32)

    PK = [ps("PK0", [128, C]), ps("PK1", [128, C])]
    PV = ps("PV", [128, C])
    PKV = ps("PKV", [128, C])

    s_xcE, s_xcO = sem("s_xcE"), sem("s_xcO")
    s_tabE, s_tabO = sem("s_tabE"), sem("s_tabO")
    s_w, s_gc = sem("s_w"), sem("s_gc")
    s_tk, s_tv, s_t2 = sem("s_tk"), sem("s_tv"), sem("s_t2")
    s_gel, s_vev = sem("s_gel"), sem("s_vev")
    s_v1, s_kve = sem("s_v1"), sem("s_kve")
    s_ksf, s_kse = sem("s_ksf"), sem("s_kse")
    s_out = sem("s_out")

    J = 2 * ntb

    xc_sem = lambda j: s_xcE if j % 2 == 0 else s_xcO
    tab_sem = lambda j: s_tabE if j % 2 == 0 else s_tabO

    with nc.Block() as block:
        @block.gpsimd
        def _(g):
            g.dma_start(out=wk_sb[:].rearrange("p (ck d) -> p ck d", ck=8), in_=wk.rearrange("(ck p) d -> p ck d", p=128)).then_inc(s_w, 16)
            g.dma_start(out=wv_sb[:].rearrange("p (ck d) -> p ck d", ck=8), in_=wv.rearrange("(ck p) d -> p ck d", p=128)).then_inc(s_w, 16)
            g.dma_start(out=bk_sb[:], in_=bias3[1:2, :]).then_inc(s_w, 16)
            g.dma_start(out=bv_sb[:], in_=bias3[2:3, :]).then_inc(s_w, 16)
            g.memset(ones1[:], 1.0)
            g.memset(ones_f[:], 1.0).then_inc(s_gc, 1)
            for j in range(J):
                b, i = divmod(j, ntb)
                if j >= 2:
                    g.wait_ge(s_tv, j - 1)
                row = (i + 1) * TILE
                src = xcm[b, :, row:row + TILE].rearrange(
                    "(ck p) t -> p ck t", p=128)
                g.dma_start(out=xbf[j % 2][:].rearrange("p (ck t) -> p ck t", ck=8),
                            in_=src).then_inc(xc_sem(j), 16)

        @block.sync
        def _(sy):
            for j in range(J):
                b, i = divmod(j, ntb)
                if j >= 2:
                    sy.wait_ge(s_v1, j - 1)
                row = (i + 1) * TILE
                sy.dma_start(out=tabt[j % 2][:], in_=tab[row:row + TILE, :]
                             ).then_inc(tab_sem(j), 16)
            sy.wait_ge(s_kve, 2)
            sy.wait_ge(s_kse, 2)
            sy.dma_start(out=kvp[:], in_=kvs_sb[:]).then_inc(s_out, 16)
            sy.dma_start(out=ksp[:], in_=ksp_sb[:]).then_inc(s_out, 16)
            sy.wait_ge(s_out, 32)

        @block.tensor
        def _(t):
            t.wait_ge(s_w, 64)
            t.wait_ge(s_gc, 1)

            def kv_mms(jj):
                bb, ii = divmod(jj, ntb)
                t.wait_ge(s_v1, jj + 1)
                t.wait_ge(s_vev, jj + 1)
                if ii == 0 and bb == 1:
                    t.wait_ge(s_kve, 1)
                for p in range(8):
                    st = (ii == 0) and (p in (0, 4))
                    sp = (ii == ntb - 1) and (p in (3, 7))
                    mm = t.matmul(PKV[:, p * 128:(p + 1) * 128],
                                  kr_bf[jj % 2][:, p * 128:(p + 1) * 128],
                                  v_bf[jj % 2][:, p * 128:(p + 1) * 128],
                                  start=st, stop=sp)
                mm.then_inc(s_t2, 1)

            for j in range(J):
                t.wait_ge(xc_sem(j), 16 * (j // 2 + 1))
                if j >= 2:
                    t.wait_ge(s_gel, j - 1)
                pk = PK[j % 2]
                xb = xbf[j % 2]
                for ck in range(8):
                    for h in range(2):
                        t.matmul(pk[:, h * 512:(h + 1) * 512],
                                 xb[:, ck * 128:(ck + 1) * 128],
                                 wk_sb[:, ck * C + h * 512: ck * C + (h + 1) * 512],
                                 start=(ck == 0), stop=False)
                for h in range(2):
                    mm = t.matmul(pk[:, h * 512:(h + 1) * 512], ones1[:],
                                  bk_sb[0:1, h * 512:(h + 1) * 512],
                                  start=False, stop=True)
                mm.then_inc(s_tk, 1)
                if j >= 1:
                    t.wait_ge(s_vev, j)
                for ck in range(8):
                    for h in range(2):
                        t.matmul(PV[:, h * 512:(h + 1) * 512],
                                 xb[:, ck * 128:(ck + 1) * 128],
                                 wv_sb[:, ck * C + h * 512: ck * C + (h + 1) * 512],
                                 start=(ck == 0), stop=False)
                for h in range(2):
                    mm = t.matmul(PV[:, h * 512:(h + 1) * 512], ones1[:],
                                  bv_sb[0:1, h * 512:(h + 1) * 512],
                                  start=False, stop=True)
                mm.then_inc(s_tv, 1)
                if j >= 1:
                    kv_mms(j - 1)
            kv_mms(J - 1)
            t.wait_ge(s_kve, 2)
            t.wait_ge(s_v1, J)
            for b in range(2):
                if b == 1:
                    t.wait_ge(s_kse, 1)
                for h in range(2):
                    mm = t.matmul(PKV[0:1, h * 512:(h + 1) * 512], ones_f[:],
                                  ksacc[b][:, h * 512:(h + 1) * 512],
                                  start=True, stop=True)
                mm.then_inc(s_ksf, 1)

        @block.scalar
        def _(s):
            for j in range(J):
                s.wait_ge(s_tk, j + 1)
                if j >= 2:
                    s.wait_ge(s_v1, j - 1)
                s.activation(k_sb[j % 2][:], PK[j % 2][:], AF.Gelu
                             ).then_inc(s_gel, 1)
                s.wait_ge(s_tv, j + 1)
                if j >= 2:
                    s.wait_ge(s_t2, j - 1)
                s.copy(v_bf[j % 2][:], PV[:]).then_inc(s_vev, 1)

        @block.vector
        def _(v):
            for j in range(J):
                b, i = divmod(j, ntb)
                v.wait_ge(s_gel, j + 1)
                v.wait_ge(tab_sem(j), 16 * (j // 2 + 1))
                if j >= 2:
                    v.wait_ge(s_t2, j - 1)
                ks = k_sb[j % 2]
                if i == 0:
                    v.tensor_copy(ksacc[b][:], ks[:])
                else:
                    v.tensor_add(ksacc[b][:], ksacc[b][:], ks[:])
                cosv = tabt[j % 2][:, 0:512]
                sinv = tabt[j % 2][:, 512:1024]
                kr = kr_bf[j % 2]
                v.scalar_tensor_tensor(t1[:], _even(ks[:]), 0.21, cosv,
                                       ALU.add, ALU.mult)
                v.scalar_tensor_tensor(t2[:], _odd(ks[:]), 0.21, sinv,
                                       ALU.add, ALU.mult)
                v.drain()
                v.tensor_sub(_even(kr[:]), t1[:], t2[:])
                v.drain()
                v.scalar_tensor_tensor(t1[:], _odd(ks[:]), 0.21, cosv,
                                       ALU.add, ALU.mult)
                v.scalar_tensor_tensor(t2[:], _even(ks[:]), 0.21, sinv,
                                       ALU.add, ALU.mult)
                v.drain()
                v.tensor_add(_odd(kr[:]), t1[:], t2[:]).then_inc(s_v1, 1)
                if i == ntb - 1:
                    v.wait_ge(s_t2, 16 * (b + 1))
                    v.tensor_scalar(kvs_sb[:, b * C:(b + 1) * C], PKV[:],
                                    SCALE2, None, ALU.mult).then_inc(s_kve, 1)
            for b in range(2):
                v.wait_ge(s_ksf, b + 1)
                v.tensor_copy(ksp_sb[0:1, b * C:(b + 1) * C], PKV[0:1, :]).then_inc(s_kse, 1)

    ctx.close()
    return nc


def build_b(tj=None, dbg=False, tj2=None):
    nc = bass.Bass("TRN2", target_bir_lowering=False,
                   detect_race_conditions=False)
    xcm = nc.dram_tensor("xcm", [B, C, TW], F32, kind="ExternalInput").ap()
    xtm = nc.dram_tensor("xtm", [B, TW, C], F32, kind="ExternalInput").ap()
    tab = nc.dram_tensor("tab", [TW, C], F32, kind="ExternalInput").ap()
    wq = nc.dram_tensor("wq", [C, C], F32, kind="ExternalInput").ap()
    wr = nc.dram_tensor("wr", [C, C], F32, kind="ExternalInput").ap()
    wkf = nc.dram_tensor("wkf", [C, C], F32, kind="ExternalInput").ap()
    wvf = nc.dram_tensor("wvf", [C, C], F32, kind="ExternalInput").ap()
    bias3 = nc.dram_tensor("bias3", [3, C], F32, kind="ExternalInput").ap()
    gb = nc.dram_tensor("gb", [2, C], F32, kind="ExternalInput").ap()
    mus = nc.dram_tensor("mus", [2, C], F32, kind="ExternalInput").ap()
    mask = nc.dram_tensor("mask", [128, 1], F32, kind="ExternalInput").ap()
    ident = nc.dram_tensor("ident", [128, 128], F32, kind="ExternalInput").ap()
    kv_all = nc.dram_tensor("kv_all", [NCORES, 128, 2 * C], F32,
                            kind="ExternalInput").ap()
    ks_in = nc.dram_tensor("ks_in", [NCORES, 2 * C], F32,
                           kind="ExternalInput").ap()
    y = nc.dram_tensor("y", [B, TLOC, C], F32, kind="ExternalOutput").ap()
    x1_dr = nc.dram_tensor("x1_dr", [B, TW, C], F32, kind="Internal").ap()
    km_dr = nc.dram_tensor("km_dr", [1, 2 * C], F32, kind="Internal").ap()
    ch_dr = nc.dram_tensor("ch_dr", [1, 32], F32, kind="Internal").ap()
    NT2 = 2 * NT
    if dbg:
        d_xr = nc.dram_tensor("d_xr", [NT2, 128, C], BF16, kind="ExternalOutput").ap()
        d_xk = nc.dram_tensor("d_xk", [NT2, 128, C], BF16, kind="ExternalOutput").ap()
        d_ka = nc.dram_tensor("d_ka", [NT2, 128, C], BF16, kind="ExternalOutput").ap()
        d_tr = nc.dram_tensor("d_tr", [NT2, 128, C], F32, kind="ExternalOutput").ap()
        d_r3 = nc.dram_tensor("d_r3", [NT2, 128, C], F32, kind="ExternalOutput").ap()
        d_xp = nc.dram_tensor("d_xp", [NT2, 128, C], F32, kind="ExternalOutput").ap()
        d_xt = nc.dram_tensor("d_xt", [NT2, 128, C], BF16, kind="ExternalOutput").ap()
        d_pt = nc.dram_tensor("d_pt", [4, 128, C], BF16, kind="ExternalOutput").ap()
        d_y1 = nc.dram_tensor("d_y1", [128, C], BF16, kind="ExternalOutput").ap()
        d_bq = nc.dram_tensor("d_bq", [4, 128, C], F32, kind="ExternalOutput").ap()
        d_bqr = nc.dram_tensor("d_bqr", [4, 128, C], BF16, kind="ExternalOutput").ap()
        d_br1 = nc.dram_tensor("d_br1", [4, 128, C], F32, kind="ExternalOutput").ap()
        d_by1 = nc.dram_tensor("d_by1", [4, 128, C], F32, kind="ExternalOutput").ap()
        d_br2 = nc.dram_tensor("d_br2", [4, 128, C], F32, kind="ExternalOutput").ap()
        d_su = nc.dram_tensor("d_su", [128, 12], F32, kind="ExternalOutput").ap()
        d_sq = nc.dram_tensor("d_sq", [128, 12], F32, kind="ExternalOutput").ap()
        d_st = nc.dram_tensor("d_st", [128, 80], F32, kind="ExternalOutput").ap()

    ctx = ExitStack()
    sb = lambda nm, shp, dt=F32: ctx.enter_context(nc.sbuf_tensor(nm, shp, dt))
    ps = lambda nm, shp, dt=F32: ctx.enter_context(nc.psum_tensor(nm, shp, dt))
    sem = lambda nm: ctx.enter_context(nc.semaphore(nm))

    wq_sb = sb("wq_sb", [128, 8 * C], BF16)
    wr_sb = wq_sb
    wkf_sb = sb("wkf_sb", [128, 8 * C], BF16)
    wvf_sb = sb("wvf_sb", [128, 8 * C], BF16)
    bq_sb = sb("bq_sb", [1, C], BF16)
    ones1 = sb("ones1", [1, 128], BF16)
    identb = sb("identb", [128, 128], BF16)
    mask_sb = sb("mask_sb", [128, 1], F32)
    kvacc = sb("kvacc", [128, 2 * C], F32)
    kv_bd = sb("kv_bd", [128, 2 * C], BF16)
    kmb = sb("kmb", [128, 2 * C], F32)
    chb = sb("chb", [128, 32], F32)
    gamb = sb("gamb", [128, C], F32)
    betb = sb("betb", [128, C], F32)
    eps12 = sb("eps12", [128, 1], F32)
    eps5 = sb("eps5", [128, 1], F32)
    dumpbuf = sb("dumpbuf", [128, C], F32)
    res3x = sb("res3x", [128, C], F32)
    xbf = [sb(f"xbf{p}", [128, 8 * TILE], BF16) for p in range(2)]
    tabt = [sb(f"tabt{p}", [128, C], F32) for p in range(2)]
    xts = [sb(f"xts{p}", [128, C], F32) for p in range(3)]
    q_sb = [sb(f"q_sb{p}", [128, C], F32) for p in range(2)]
    zscr = [sb(f"zscr{p}", [128, 16 * 65], F32) for p in range(2)]
    z16 = [sb(f"z16_{p}", [128, 16], F32) for p in range(2)]
    z16r = [sb(f"z16r{p}", [128, 16], F32) for p in range(2)]
    qro = [sb(f"qro{p}", [128, C], F32) for p in range(3)]
    ropescr = sb("ropescr", [128, C], F32)
    qr_bf = [sb(f"qr_bf{p}", [128, C], BF16) for p in range(2)]
    qrT_sb = [sb(f"qrT_sb{p}", [128, C], BF16) for p in range(2)]
    res1 = [sb(f"res1_{p}", [128, C], F32) for p in range(4)]
    y1_sb = [sb(f"y1_sb{p}", [128, C], F32) for p in range(2)]
    xbg = [sb(f"xbg{p}", [128, C], F32) for p in range(2)]
    res2 = [sb(f"res2_{p}", [128, C], F32) for p in range(4)]
    x1_sb = [sb(f"x1_sb{p}", [128, C], F32) for p in range(2)]
    sumst = sb("sumst", [128, 12], F32)
    sqst = sb("sqst", [128, 12], F32)
    stt_ = sb("stt_", [128, 80], F32)

    x1t = q_sb
    xp = [zscr[0][:, 0:C], zscr[1][:, 0:C]]
    xt2 = [res1[0], res1[1]]
    d_sb = [qro[0], qro[1]]
    dmscr = ropescr
    ugscr = xbg[0][:]
    xr_bf = qr_bf
    xk_bf = qrT_sb
    xrT_sb = [y1_sb[p][:].bitcast(BF16)[:, 0:C] for p in range(2)]
    xkT_sb = [res2[p][:].bitcast(BF16)[:, 0:C] for p in range(2)]
    kact_bf = [x1_sb[p][:].bitcast(BF16)[:, 0:C] for p in range(2)]
    kactT_sb = [x1_sb[p][:].bitcast(BF16)[:, C:2 * C] for p in range(2)]
    krscr = xts[0]
    foscr = xts[1]
    tr5 = [tabt[0][:], tabt[1][:], xbg[1][:], kv_bd[:].bitcast(F32)[:, 0:C],
           xts[2][:]]
    res3 = [kmb[:, 0:C], kmb[:, C:2 * C], qro[2][:, 0:C], res3x[:]]
    y_sb = [kvacc[:, 0:C], kvacc[:, C:2 * C]]
    murb = gamb
    mukb = betb

    PQ = [ps("PQ0", [128, C]), ps("PQ1", [128, C])]
    PT1 = ps("PT1", [128, C], BF16)
    PO = ps("PO", [128, C])
    PTb = ps("PTb", [128, C], BF16)
    PR, PK, PV, PTa = PQ[0], PQ[1], PO, PT1

    names = ["sTQ", "sSG", "sZr", "sVZ", "sTT", "sEQ", "sTO", "sR1",
             "sS1", "sS3", "sS5", "sVb1", "sVb2", "sVb3",
             "sS2", "sSb1", "sVa1", "sR2", "sSb2", "sVc2", "sS4", "sGX",
             "sGD", "sGM", "sVM", "sTX2", "sEX", "sTR", "sTK", "sSR", "sSK",
             "sTKT", "sEK", "sTV", "sVG", "sSb3", "sVc3", "sS6"]
    S = {nm: sem(nm) for nm in names}
    s_xcE, s_xcO = sem("s_xcE"), sem("s_xcO")
    s_tabE, s_tabO = sem("s_tabE"), sem("s_tabO")
    s_xtE, s_xtO = sem("s_xtE"), sem("s_xtO")
    s_x1E, s_x1O = sem("s_x1E"), sem("s_x1O")
    s_xpE, s_xpO = sem("s_xpE"), sem("s_xpO")
    s_x2E, s_x2O = sem("s_x2E"), sem("s_x2O")
    s_w = sem("s_w")
    s_w2 = sem("s_w2")
    s_w3 = sem("s_w3")
    s_acc = sem("s_acc")
    s_ack = sem("s_ack")
    s_set = sem("s_set")
    s_kmb = sem("s_kmb")
    s_gc = sem("s_gc")
    s_y = sem("s_y")
    s_xs = sem("s_xs")
    s_pb = sem("s_pb")

    def stc(j, ln, k):
        base = (j % 4) * 20 + ln * 6 + k
        return stt_[:, base:base + 1]

    def sucol(j, ln):
        c0 = (j % 4) * 3 + ln
        return sumst[:, c0:c0 + 1]

    def sqcol(j, ln):
        c0 = (j % 4) * 3 + ln
        return sqst[:, c0:c0 + 1]

    xc_sem = lambda j: s_xcE if j % 2 == 0 else s_xcO
    tab_sem = lambda j: s_tabE if j % 2 == 0 else s_tabO
    xt_sem = lambda j: s_xtE if j % 2 == 0 else s_xtO
    x1_sem = lambda j: s_x1E if j % 2 == 0 else s_x1O
    xp_sem = lambda j: s_xpE if j % 2 == 0 else s_xpO
    x2_sem = lambda j: s_x2E if j % 2 == 0 else s_x2O

    def dwait(eng, sp, j):
        eng.wait_ge(sp(j), 16 * (j // 2 + 1))

    tiles = [(b, i) for i in range(NT + 1) for b in range(2)]
    if tj is not None:
        tiles = tiles[:tj]
    J = len(tiles)
    tiles2 = [(b, i) for i in range(1, NT + 1) for b in range(2)]
    if tj2 is not None:
        tiles2 = tiles2[:tj2]
    if tj is not None:
        tiles2 = []
    J2 = len(tiles2)

    def emit(stages, nt):
        if nt == 0:
            return
        lo = min(off for off, _ in stages)
        hi = max(off for off, _ in stages)
        for r in range(lo, nt + hi):
            for off, fn in stages:
                j = r - off
                if 0 <= j < nt:
                    fn(j)

    with nc.Block() as block:
        @block.gpsimd
        def _(g):
            g.dma_start(out=wq_sb[:].rearrange("p (ck d) -> p ck d", ck=8), in_=wq.rearrange("(ck p) d -> p ck d", p=128)).then_inc(s_w, 16)
            g.dma_start(out=wkf_sb[:].rearrange("p (ck d) -> p ck d", ck=8), in_=wkf.rearrange("(ck p) d -> p ck d", p=128)).then_inc(s_w, 16)
            g.dma_start(out=wvf_sb[:].rearrange("p (ck d) -> p ck d", ck=8), in_=wvf.rearrange("(ck p) d -> p ck d", p=128)).then_inc(s_w, 16)
            g.dma_start(out=bq_sb[:], in_=bias3[0:1, :]).then_inc(s_w, 16)
            g.dma_start(out=identb[:], in_=ident[:]).then_inc(s_w, 16)
            g.dma_start(out=gamb[:], in_=gb[0:1, :].broadcast_to([128, C])).then_inc(s_w, 16)
            g.dma_start(out=betb[:], in_=gb[1:2, :].broadcast_to([128, C])).then_inc(s_w, 16)
            g.memset(ones1[:], 1.0)
            g.memset(eps12[:], 1e-12)
            g.memset(eps5[:], 1e-5).then_inc(s_gc, 1)
            g.dma_start(out=kvacc[:], in_=kv_all[0]).then_inc(s_acc, 16)
            for k in range(1, NCORES):
                g.wait_ge(s_acc, 16 * k)
                g.dma_start(out=kvacc[:], in_=kv_all[k], accum_op=ALU.add
                            ).then_inc(s_acc, 16)
            g.dma_start(out=kmb[0:1, :], in_=ks_in[0:1, :]).then_inc(s_ack, 16)
            for k in range(1, NCORES):
                g.wait_ge(s_ack, 16 * k)
                g.dma_start(out=kmb[0:1, :], in_=ks_in[k:k + 1, :],
                            accum_op=ALU.add).then_inc(s_ack, 16)
            g.wait_ge(s_set, 1)
            g.dma_start(out=km_dr[:], in_=kmb[0:1, :]).then_inc(s_kmb, 16)
            g.dma_start(out=ch_dr[:], in_=chb[0:1, 0:32]).then_inc(s_kmb, 16)
            g.wait_ge(s_kmb, 32)
            g.dma_start(out=kmb[:, 0:C], in_=km_dr[0:1, 0:C].broadcast_to([128, C])).then_inc(s_kmb, 16)
            g.dma_start(out=kmb[:, C:2 * C], in_=km_dr[0:1, C:2 * C].broadcast_to([128, C])).then_inc(s_kmb, 16)
            g.dma_start(out=chb[:, 0:16], in_=ch_dr[0:1, 0:16].broadcast_to([128, 16])).then_inc(s_kmb, 16)
            g.dma_start(out=chb[:, 16:32], in_=ch_dr[0:1, 16:32].broadcast_to([128, 16])).then_inc(s_kmb, 16)

            def g_xbg(j):
                dwait(g, xt_sem, j)
                if j >= 2:
                    g.wait_ge(S["sR2"], j - 1)
                g.tensor_add(xbg[j % 2][:], xts[j % 3][:], betb[:]
                             ).then_inc(S["sGX"], 1)

            def g_xbf(j):
                b, i = tiles[j]
                if j >= 2:
                    g.wait_ge(S["sTQ"], j - 1)
                src = xcm[b, :, i * TILE:(i + 1) * TILE].rearrange(
                    "(ck p) t -> p ck t", p=128)
                g.dma_start(out=xbf[j % 2][:].rearrange("p (ck t) -> p ck t", ck=8),
                            in_=src).then_inc(xc_sem(j), 16)

            emit([(10, g_xbg), (-2, g_xbf)], J)
            g.nop().then_inc(s_pb, 1)
            g.wait_ge(s_pb, 5)
            g.dma_start(out=murb[:], in_=mus[0:1, :].broadcast_to([128, C])).then_inc(s_w3, 16)
            g.dma_start(out=mukb[:], in_=mus[1:2, :].broadcast_to([128, C])).then_inc(s_w3, 16)
            g.dma_start(out=wr_sb[:].rearrange("p (ck d) -> p ck d", ck=8), in_=wr.rearrange("(ck p) d -> p ck d", p=128)).then_inc(s_w3, 16)
            g.wait_ge(s_w3, 48)

            def g_d(j):
                dwait(g, x1_sem, j)
                dwait(g, xp_sem, j)
                if j >= 2:
                    g.wait_ge(S["sVM"], j - 1)
                g.tensor_sub(d_sb[j % 2][:], x1t[j % 2][:], xp[j % 2][:]
                             ).then_inc(S["sGD"], 1)

            def g_ugxk(j):
                if j >= 2:
                    g.wait_ge(S["sTX2"], j - 1)
                g.tensor_mul(ugscr, d_sb[j % 2][:], mukb[:])
                g.tensor_add(xk_bf[j % 2][:], xp[j % 2][:], ugscr
                             ).then_inc(S["sGM"], 1)

            emit([(0, g_d), (1, g_ugxk)], J2)

        @block.sync
        def _(sy):
            sy.dma_start(out=mask_sb[:], in_=mask[:]).then_inc(s_w2, 16)

            def s_tab(j):
                b, i = tiles[j]
                if j >= 2:
                    sy.wait_ge(S["sZr"], j - 1)
                sy.dma_start(out=tabt[j % 2][:], in_=tab[i * TILE:(i + 1) * TILE, :]
                             ).then_inc(tab_sem(j), 16)

            def s_xts(j):
                b, i = tiles[j]
                if j >= 3:
                    sy.wait_ge(S["sGX"], j - 2)
                    sy.wait_ge(S["sR1"], j - 2)
                sy.dma_start(out=xts[j % 3][:],
                             in_=xtm[b, i * TILE:(i + 1) * TILE, :]
                             ).then_inc(xt_sem(j), 16)

            def s_x1(j):
                b, i = tiles[j]
                sy.wait_ge(S["sS4"], j + 1)
                sy.dma_start(out=x1_dr[b, i * TILE:(i + 1) * TILE, :],
                             in_=x1_sb[j % 2][:]).then_inc(s_xs, 16)

            stages1 = [(0, s_tab), (7, s_xts), (16, s_x1)]
            if dbg:
                s_db1 = sem("s_db1")

                def db_q(j):
                    if j < 4:
                        sy.wait_ge(S["sZr"], j + 1)
                        sy.dma_start(out=d_bq[j], in_=q_sb[j % 2][:]).then_inc(s_db1, 16)

                def db_qr(j):
                    if j < 4:
                        sy.wait_ge(S["sTT"], j + 1)
                        sy.dma_start(out=d_bqr[j], in_=qr_bf[j % 2][:]).then_inc(s_db1, 16)

                def db_r1(j):
                    if j < 4:
                        sy.wait_ge(S["sS2"], j + 1)
                        sy.dma_start(out=d_br1[j], in_=res1[j % 4][:]).then_inc(s_db1, 16)
                        sy.dma_start(out=d_by1[j], in_=y1_sb[j % 2][:]).then_inc(s_db1, 16)

                def db_r2(j):
                    if j < 4:
                        sy.wait_ge(S["sS4"], j + 1)
                        sy.dma_start(out=d_br2[j], in_=res2[j % 4][:]).then_inc(s_db1, 16)

                stages1 += [(3, db_q), (6, db_qr), (11, db_r1), (13, db_r2)]
            emit(stages1, J)
            if dbg:
                sy.wait_ge(s_xs, 16 * J)
                sy.dma_start(out=d_su, in_=sumst[:]).then_inc(s_db1, 16)
                sy.dma_start(out=d_sq, in_=sqst[:]).then_inc(s_db1, 16)
                sy.dma_start(out=d_st, in_=stt_[:]).then_inc(s_db1, 16)
                sy.wait_ge(s_db1, 16 * (5 * min(J, 4) + 3))
            sy.nop().then_inc(s_pb, 1)
            sy.wait_ge(s_pb, 5)
            sy.wait_ge(s_xs, 16 * J)

            def s_x1t(j):
                b, i = tiles2[j]
                if j >= 2:
                    sy.wait_ge(S["sGD"], j - 1)
                sy.dma_start(out=x1t[j % 2][:],
                             in_=x1_dr[b, i * TILE:(i + 1) * TILE, :]
                             ).then_inc(x1_sem(j), 16)

            def s_xp(j):
                b, i = tiles2[j]
                if j >= 2:
                    sy.wait_ge(S["sGM"], j - 1)
                    sy.wait_ge(S["sVM"], j - 1)
                sy.dma_start(out=xp[j % 2],
                             in_=x1_dr[b, i * TILE - 1:(i + 1) * TILE - 1, :]
                             ).then_inc(xp_sem(j), 16)

            def s_xt2(j):
                b, i = tiles2[j]
                if j >= 2:
                    sy.wait_ge(S["sVG"], j - 1)
                sy.dma_start(out=xt2[j % 2][:],
                             in_=x1_dr[b, i * TILE:(i + 1) * TILE, :]
                             ).then_inc(x2_sem(j), 16)

            def s_ystore(j):
                b, i = tiles2[j]
                sy.wait_ge(S["sS6"], j + 1)
                sy.dma_start(out=y[b, (i - 1) * TILE:i * TILE, :],
                             in_=y_sb[j % 2]).then_inc(s_y, 16)

            stages2 = [(-2, s_x1t), (-1, s_xp), (6, s_xt2), (12, s_ystore)]
            if dbg:
                s_dbg = sem("s_dbg")

                def dmp_xp(j):
                    sy.wait_ge(S["sGD"], j + 1)
                    sy.dma_start(out=d_xp[j], in_=xp[j % 2]).then_inc(s_dbg, 16)

                def dmp_xr(j):
                    sy.wait_ge(S["sVM"], j + 1)
                    sy.dma_start(out=d_xr[j], in_=xr_bf[j % 2][:]).then_inc(s_dbg, 16)
                    sy.wait_ge(S["sGM"], j + 1)
                    sy.dma_start(out=d_xk[j], in_=xk_bf[j % 2][:]).then_inc(s_dbg, 16)

                def dmp_ka(j):
                    sy.wait_ge(S["sSK"], j + 1)
                    sy.dma_start(out=d_ka[j], in_=kact_bf[j % 2]).then_inc(s_dbg, 16)
                    sy.wait_ge(S["sSR"], j + 1)
                    sy.dma_start(out=d_tr[j], in_=tr5[j % 5]).then_inc(s_dbg, 16)

                def dmp_r3(j):
                    sy.wait_ge(S["sVG"], j + 1)
                    sy.dma_start(out=d_r3[j], in_=res3[j % 4]).then_inc(s_dbg, 16)

                def dmp_xt(j):
                    sy.wait_ge(S["sEX"], j + 1)
                    sy.dma_start(out=d_xt[j], in_=xrT_sb[j % 2]).then_inc(s_dbg, 16)


                stages2 += [(0, dmp_xp), (1, dmp_xr), (4, dmp_ka), (9, dmp_r3),
                            (2, dmp_xt)]
            emit(stages2, J2)
            if J2 > 0:
                sy.wait_ge(s_y, 16 * J2)
                if dbg:
                    sy.wait_ge(s_dbg, 16 * 7 * J2)

        @block.tensor
        def _(t):
            t.wait_ge(s_w, 16 * 7)
            t.wait_ge(s_gc, 1)

            def t_qproj(j):
                dwait(t, xc_sem, j)
                if j >= 2:
                    t.wait_ge(S["sSG"], j - 1)
                xbfj = xbf[j % 2]
                for ck in range(8):
                    for h in range(2):
                        t.matmul(PQ[j % 2][:, h * 512:(h + 1) * 512],
                                 xbfj[:, ck * 128:(ck + 1) * 128],
                                 wq_sb[:, ck * C + h * 512: ck * C + (h + 1) * 512],
                                 start=(ck == 0), stop=False)
                for h in range(2):
                    mm = t.matmul(PQ[j % 2][:, h * 512:(h + 1) * 512], ones1[:],
                                  bq_sb[0:1, h * 512:(h + 1) * 512],
                                  start=False, stop=True)
                mm.then_inc(S["sTQ"], 1)

            def t_trans(j):
                t.wait_ge(S["sVZ"], j + 1)
                if j >= 1:
                    t.wait_ge(S["sEQ"], j)
                for ck in range(8):
                    mm = t.transpose(PT1[:, ck * 128:(ck + 1) * 128],
                                     qr_bf[j % 2][:, ck * 128:(ck + 1) * 128],
                                     identb[:])
                mm.then_inc(S["sTT"], 1)

            def t_out(j):
                b, i = tiles[j]
                t.wait_ge(S["sEQ"], j + 1)
                if j >= 1:
                    t.wait_ge(S["sR1"], j)
                if j < 2:
                    t.wait_ge(s_set, 2)
                for p in range(8):
                    mm = t.matmul(PO[:, p * 128:(p + 1) * 128],
                                  qrT_sb[j % 2][:, p * 128:(p + 1) * 128],
                                  kv_bd[:, b * C + p * 128: b * C + (p + 1) * 128],
                                  start=(p in (0, 4)), stop=(p in (3, 7)))
                mm.then_inc(S["sTO"], 1)

            emit([(0, t_qproj), (5, t_trans), (7, t_out)], J)
            t.transpose(PTb[:, 0:128], identb[:], identb[:]).then_inc(s_pb, 1)
            t.wait_ge(s_pb, 5)
            t.wait_ge(s_w3, 48)

            def t_xrxk(j):
                t.wait_ge(S["sVM"], j + 1)
                if j >= 1:
                    t.wait_ge(S["sEX"], j)
                if j >= 4:
                    t.wait_ge(S["sEK"], j - 3)
                for ck in range(8):
                    t.transpose(PTa[:, ck * 128:(ck + 1) * 128],
                                xr_bf[j % 2][:, ck * 128:(ck + 1) * 128],
                                identb[:])
                t.wait_ge(S["sGM"], j + 1)
                for ck in range(8):
                    mm = t.transpose(PTb[:, ck * 128:(ck + 1) * 128],
                                     xk_bf[j % 2][:, ck * 128:(ck + 1) * 128],
                                     identb[:])
                mm.then_inc(S["sTX2"], 1)

            def t_rk(j):
                t.wait_ge(S["sEX"], j + 1)
                if j >= 1:
                    t.wait_ge(S["sSR"], j)
                for ck in range(8):
                    for h in range(2):
                        mm = t.matmul(PR[:, h * 512:(h + 1) * 512],
                                      xrT_sb[j % 2][:, ck * 128:(ck + 1) * 128],
                                      wr_sb[:, ck * C + h * 512: ck * C + (h + 1) * 512],
                                      start=(ck == 0), stop=(ck == 7))
                mm.then_inc(S["sTR"], 1)
                if j >= 1:
                    t.wait_ge(S["sSK"], j)
                for ck in range(8):
                    for h in range(2):
                        mm = t.matmul(PK[:, h * 512:(h + 1) * 512],
                                      xkT_sb[j % 2][:, ck * 128:(ck + 1) * 128],
                                      wkf_sb[:, ck * C + h * 512: ck * C + (h + 1) * 512],
                                      start=(ck == 0), stop=(ck == 7))
                mm.then_inc(S["sTK"], 1)

            def t_kactT(j):
                t.wait_ge(S["sSK"], j + 1)
                t.wait_ge(S["sEX"], min(j + 4, J2))
                if j >= 1:
                    t.wait_ge(S["sEK"], j)
                for ck in range(8):
                    mm = t.transpose(PTa[:, ck * 128:(ck + 1) * 128],
                                     kact_bf[j % 2][:, ck * 128:(ck + 1) * 128],
                                     identb[:])
                mm.then_inc(S["sTKT"], 1)

            def t_vout(j):
                t.wait_ge(S["sEK"], j + 1)
                if j >= 1:
                    t.wait_ge(S["sVG"], j)
                for ck in range(8):
                    for h in range(2):
                        mm = t.matmul(PV[:, h * 512:(h + 1) * 512],
                                      kactT_sb[j % 2][:, ck * 128:(ck + 1) * 128],
                                      wvf_sb[:, ck * C + h * 512: ck * C + (h + 1) * 512],
                                      start=(ck == 0), stop=(ck == 7))
                mm.then_inc(S["sTV"], 1)

            emit([(2, t_xrxk), (3, t_rk), (5, t_kactT), (7, t_vout)], J2)

        @block.scalar
        def _(s):
            s.wait_ge(s_gc, 1)
            s.wait_ge(s_w2, 16)

            def s_gelu(j):
                s.wait_ge(S["sTQ"], j + 1)
                if j >= 2:
                    s.wait_ge(S["sZr"], j - 1)
                s.activation(q_sb[j % 2][:], PQ[j % 2][:], AF.Gelu
                             ).then_inc(S["sSG"], 1)

            def s_evac(j):
                s.wait_ge(S["sTT"], j + 1)
                if j >= 2:
                    s.wait_ge(S["sTO"], j - 1)
                s.copy(qrT_sb[j % 2][:], PT1[:]).then_inc(S["sEQ"], 1)

            def s_ln1b(j):
                s.wait_ge(S["sVb1"], j + 1)
                if j >= 2:
                    s.wait_ge(S["sR2"], j - 1)
                s.activation(y1_sb[j % 2][:], res1[j % 4][:], AF.Identity,
                             bias=stc(j, 0, 4), scale=stc(j, 0, 3)
                             ).then_inc(S["sS2"], 1)

            def s_ln1a(j):
                s.wait_ge(S["sR1"], j + 1)
                s.activation(dumpbuf[:], res1[j % 4][:], AF.Square,
                             accum_out=sqcol(j, 0))
                s.activation(stc(j, 0, 0), sucol(j, 0), AF.Square, scale=INV_C
                             ).then_inc(S["sS1"], 1)

            def s_ln2c(j):
                real = tiles[j][1] > 0
                s.wait_ge(S["sVb2"], j + 1)
                if j >= 2:
                    s.wait_ge(s_xs, 16 * (j - 1))
                if real:
                    s.activation(x1_sb[j % 2][:], res2[j % 4][:], AF.Identity,
                                 bias=stc(j, 1, 4), scale=stc(j, 1, 3)
                                 ).then_inc(S["sS4"], 1)
                else:
                    s.activation(x1_sb[j % 2][:], res2[j % 4][:], AF.Identity,
                                 bias=stc(j, 1, 4), scale=stc(j, 1, 3))
                    s.activation(x1_sb[j % 2][:], x1_sb[j % 2][:], AF.Identity,
                                 scale=mask_sb[:, 0:1]).then_inc(S["sS4"], 1)

            def s_ln2a(j):
                s.wait_ge(S["sR2"], j + 1)
                s.activation(dumpbuf[:], res2[j % 4][:], AF.Square,
                             accum_out=sqcol(j, 1))
                s.activation(stc(j, 1, 0), sucol(j, 1), AF.Square, scale=INV_C
                             ).then_inc(S["sS3"], 1)

            def s_sqrt1(j):
                s.wait_ge(S["sVa1"], j + 1)
                s.activation(stc(j, 0, 2), stc(j, 0, 1), AF.Sqrt,
                             bias=eps12[:, 0:1]).then_inc(S["sSb1"], 1)

            def s_sqrt2(j):
                s.wait_ge(S["sVc2"], j + 1)
                s.activation(stc(j, 1, 2), stc(j, 1, 1), AF.Sqrt,
                             bias=eps5[:, 0:1]).then_inc(S["sSb2"], 1)

            emit([(10, s_sqrt1), (14, s_sqrt2), (1, s_gelu), (6, s_evac),
                  (11, s_ln1b), (9, s_ln1a), (13, s_ln2a), (15, s_ln2c)], J)
            s.nop().then_inc(s_pb, 1)
            s.wait_ge(s_pb, 5)

            def s_ek(j):
                s.wait_ge(S["sTKT"], j + 1)
                if j >= 2:
                    s.wait_ge(S["sTV"], j - 1)
                s.copy(kactT_sb[j % 2], PTa[:]).then_inc(S["sEK"], 1)

            def s_tanh(j):
                s.wait_ge(S["sTR"], j + 1)
                if j >= 5:
                    s.wait_ge(S["sVG"], j - 4)
                s.activation(tr5[j % 5], PR[:], AF.Tanh, scale=0.5
                             ).then_inc(S["sSR"], 1)

            def s_krelu(j):
                s.wait_ge(S["sTK"], j + 1)
                if j >= 2:
                    s.wait_ge(S["sTKT"], j - 1)
                s.activation(krscr[:], PK[:], AF.Relu)
                s.activation(kact_bf[j % 2], krscr[:], AF.Square,
                             scale=RS_HALF).then_inc(S["sSK"], 1)

            def s_y2(j):
                s.wait_ge(S["sVb3"], j + 1)
                if j >= 2:
                    s.wait_ge(s_y, 16 * (j - 1))
                s.activation(y_sb[j % 2], res3[j % 4], AF.Identity,
                             bias=stc(j, 2, 4), scale=stc(j, 2, 3)
                             ).then_inc(S["sS6"], 1)

            def s_sq3(j):
                s.wait_ge(S["sVG"], j + 1)
                s.activation(dumpbuf[:], res3[j % 4], AF.Square,
                             accum_out=sqcol(j, 2))

            def s_m2c(j):
                s.activation(stc(j, 2, 0), sucol(j, 2), AF.Square, scale=INV_C
                             ).then_inc(S["sS5"], 1)

            def s_sqrt3(j):
                s.wait_ge(S["sVc3"], j + 1)
                s.activation(stc(j, 2, 2), stc(j, 2, 1), AF.Sqrt,
                             bias=eps5[:, 0:1]).then_inc(S["sSb3"], 1)

            emit([(10, s_sqrt3), (6, s_ek), (4, s_tanh), (9, s_m2c),
                  (4, s_krelu), (11, s_y2), (8, s_sq3)], J2)

        @block.vector
        def _(v):
            v.wait_ge(s_acc, 16 * NCORES)
            v.wait_ge(s_ack, 16 * NCORES)
            v.tensor_scalar(kmb[0:1, :], kmb[0:1, :], 1.0 / float(N), 0.21,
                            ALU.mult, ALU.add)
            v.drain()
            v.reduce_sum(chb[0:1, 0:32],
                         kmb[0:1, :].rearrange("p (h e) -> p h e", h=32),
                         axis=AX.X)
            v.drain()
            v.tensor_scalar(chb[0:1, 0:32], chb[0:1, 0:32], 0.21, None,
                            ALU.mult).then_inc(s_set, 1)
            v.memset(kv_bd[:], 0.0)
            v.drain()
            for bb in range(2):
                for p in range(8):
                    off = bb * C + p * 128
                    v.tensor_copy(kv_bd[0:64, off:off + 64],
                                  kvacc[0:64, off:off + 64])
                    v.tensor_copy(kv_bd[64:128, off + 64:off + 128],
                                  kvacc[64:128, off + 64:off + 128])
            v.nop().then_inc(s_set, 1)
            v.wait_ge(s_kmb, 16 * 6)
            v.tensor_scalar(chb[:], chb[:], 1e-6, None, ALU.add)
            v.drain()
            for bb in range(2):
                v.tensor_copy(
                    zscr[bb][:].rearrange("p (h e) -> p h e", e=65)[:, :, 64:65],
                    chb[:, bb * 16:(bb + 1) * 16].unsqueeze(2))
            v.drain()
            v.nop().then_inc(s_set, 1)

            def v_var1(j):
                v.wait_ge(S["sS1"], j + 1)
                v.tensor_scalar(stc(j, 0, 1), sqcol(j, 0), INV_C,
                                stc(j, 0, 0), ALU.mult, ALU.subtract
                                ).then_inc(S["sVa1"], 1)

            def v_recip1(j):
                v.wait_ge(S["sSb1"], j + 1)
                v.reciprocal(stc(j, 0, 3), stc(j, 0, 2))

            def v_negm1(j):
                v.tensor_scalar(stc(j, 0, 4), sucol(j, 0), stc(j, 0, 3),
                                -INV_C, ALU.mult, ALU.mult
                                ).then_inc(S["sVb1"], 1)

            def v_var2(j):
                v.wait_ge(S["sS3"], j + 1)
                v.tensor_scalar(stc(j, 1, 1), sqcol(j, 1), INV_C,
                                stc(j, 1, 0), ALU.mult, ALU.subtract
                                ).then_inc(S["sVc2"], 1)

            def v_recip2(j):
                v.wait_ge(S["sSb2"], j + 1)
                v.reciprocal(stc(j, 1, 3), stc(j, 1, 2))

            def v_negm2(j):
                v.tensor_scalar(stc(j, 1, 4), sucol(j, 1), stc(j, 1, 3),
                                -INV_C, ALU.mult, ALU.mult
                                ).then_inc(S["sVb2"], 1)

            def v_zrec(j):
                v.reciprocal(z16r[j % 2][:], z16[j % 2][:])

            def v_V1b(j):
                if j >= 2:
                    v.wait_ge(S["sTT"], j - 1)
                zview = z16r[j % 2][:].unsqueeze(2).broadcast_to([128, 16, 64])
                v.tensor_tensor(qr_bf[j % 2][:].rearrange("p (h e) -> p h e", h=16),
                                qro[j % 3][:].rearrange("p (h e) -> p h e", h=16),
                                zview, ALU.mult).then_inc(S["sVZ"], 1)

            def v_V1a(j):
                b = j % 2
                v.wait_ge(S["sSG"], j + 1)
                dwait(v, tab_sem, j)
                zv = zscr[b][:].rearrange("p (h e) -> p h e", e=65)
                v.tensor_tensor(zv[:, :, 0:64],
                                q_sb[j % 2][:].rearrange("p (h e) -> p h e", h=16),
                                kmb[:, b * C:(b + 1) * C].rearrange(
                                    "p (h e) -> p h e", h=16), ALU.mult)
                v.reduce_sum(z16[j % 2][:], zv, axis=AX.X)
                cosv = tabt[j % 2][:, 0:512]
                sinv = tabt[j % 2][:, 512:1024]
                qq = q_sb[j % 2]
                qo = qro[j % 3]
                t1v = ropescr[:, 0:512]
                t2v = ropescr[:, 512:1024]
                v.scalar_tensor_tensor(t1v, _even(qq[:]), 0.21, cosv,
                                       ALU.add, ALU.mult)
                v.scalar_tensor_tensor(t2v, _odd(qq[:]), 0.21, sinv,
                                       ALU.add, ALU.mult)
                v.tensor_sub(_even(qo[:]), t1v, t2v)
                v.scalar_tensor_tensor(t1v, _odd(qq[:]), 0.21, cosv,
                                       ALU.add, ALU.mult)
                v.scalar_tensor_tensor(t2v, _even(qq[:]), 0.21, sinv,
                                       ALU.add, ALU.mult)
                v.tensor_add(_odd(qo[:]), t1v, t2v).then_inc(S["sZr"], 1)

            def v_V2(j):
                v.wait_ge(S["sTO"], j + 1)
                dwait(v, xt_sem, j)
                if j >= 4:
                    v.wait_ge(S["sS2"], j - 3)
                v.scalar_tensor_tensor(res1[j % 4][:], PO[:], 1.0,
                                       xts[j % 3][:], ALU.mult, ALU.add,
                                       accum_out=sucol(j, 0)).then_inc(S["sR1"], 1)

            def v_V3(j):
                v.wait_ge(S["sS2"], j + 1)
                v.wait_ge(S["sGX"], j + 1)
                if j >= 4:
                    v.wait_ge(S["sS4"], j - 3)
                v.tensor_mul(y1_sb[j % 2][:], y1_sb[j % 2][:], gamb[:])
                v.scalar_tensor_tensor(res2[j % 4][:], y1_sb[j % 2][:], 1.0,
                                       xbg[j % 2][:], ALU.mult, ALU.add,
                                       accum_out=sucol(j, 1)).then_inc(S["sR2"], 1)

            emit([(10, v_recip1), (11, v_negm1), (14, v_recip2),
                  (15, v_negm2), (5, v_V1b), (2, v_V1a), (3, v_zrec),
                  (8, v_V2), (12, v_V3), (9, v_var1), (13, v_var2)], J)
            v.nop().then_inc(s_pb, 1)
            v.wait_ge(s_pb, 5)
            v.wait_ge(s_w3, 48)

            def v_var3(j):
                v.wait_ge(S["sS5"], j + 1)
                v.tensor_scalar(stc(j, 2, 1), sqcol(j, 2), INV_C,
                                stc(j, 2, 0), ALU.mult, ALU.subtract
                                ).then_inc(S["sVc3"], 1)

            def v_recip3(j):
                v.wait_ge(S["sSb3"], j + 1)
                v.reciprocal(stc(j, 2, 3), stc(j, 2, 2))

            def v_negm3(j):
                v.tensor_scalar(stc(j, 2, 4), sucol(j, 2), stc(j, 2, 3),
                                -INV_C, ALU.mult, ALU.mult
                                ).then_inc(S["sVb3"], 1)

            def v_dmxr(j):
                v.wait_ge(S["sGD"], j + 1)
                if j >= 2:
                    v.wait_ge(S["sTX2"], j - 1)
                v.tensor_mul(dmscr[:], d_sb[j % 2][:], murb[:])
                v.tensor_add(xr_bf[j % 2][:], xp[j % 2], dmscr[:]
                             ).then_inc(S["sVM"], 1)

            def v_fores3(j):
                v.wait_ge(S["sTV"], j + 1)
                v.wait_ge(S["sSR"], j + 1)
                dwait(v, x2_sem, j)
                if j >= 4:
                    v.wait_ge(S["sS6"], j - 3)
                v.scalar_tensor_tensor(foscr[:], tr5[j % 5], 1.0, PV[:],
                                       ALU.add, ALU.mult)
                v.scalar_tensor_tensor(res3[j % 4], foscr[:], 1.0,
                                       xt2[j % 2][:], ALU.mult, ALU.add,
                                       accum_out=sucol(j, 2)).then_inc(S["sVG"], 1)

            def v_evacs(j):
                v.wait_ge(S["sTX2"], j + 1)
                if j >= 2:
                    v.wait_ge(S["sTR"], j - 1)
                v.tensor_copy(xrT_sb[j % 2], PTa[:])
                if j >= 2:
                    v.wait_ge(S["sTK"], j - 1)
                v.tensor_copy(xkT_sb[j % 2], PTb[:]).then_inc(S["sEX"], 1)

            emit([(10, v_recip3), (11, v_negm3), (1, v_dmxr),
                  (8, v_fores3), (2, v_evacs), (9, v_var3)], J2)

    ctx.close()
    return nc


def kernel(L0_attn_x, L0_attn_Wq, L0_attn_bq, L0_attn_Wk, L0_attn_bk,
           L0_attn_Wv, L0_attn_bv, L0_attn_gamma, L0_attn_beta,
           L0_ffn_W_r, L0_ffn_W_k, L0_ffn_W_v, L0_ffn_mu_r, L0_ffn_mu_k):
    x = np.ascontiguousarray(np.asarray(L0_attn_x, dtype=np.float32))
    wqT = np.ascontiguousarray(np.asarray(L0_attn_Wq, np.float32).T)
    wkT = np.ascontiguousarray(np.asarray(L0_attn_Wk, np.float32).T)
    wvT = np.ascontiguousarray(np.asarray(L0_attn_Wv, np.float32).T)
    wrT = np.ascontiguousarray(np.asarray(L0_ffn_W_r, np.float32).T)
    wkfT = np.ascontiguousarray(np.asarray(L0_ffn_W_k, np.float32).T)
    wvfT = np.ascontiguousarray(np.asarray(L0_ffn_W_v, np.float32).T)
    bias3 = np.ascontiguousarray(np.stack([
        np.asarray(L0_attn_bq, np.float32),
        np.asarray(L0_attn_bk, np.float32),
        np.asarray(L0_attn_bv, np.float32)]))
    gbv = np.ascontiguousarray(np.stack([
        np.asarray(L0_attn_gamma, np.float32),
        np.asarray(L0_attn_beta, np.float32)]))
    mus = np.ascontiguousarray(np.stack([
        np.asarray(L0_ffn_mu_r, np.float32),
        np.asarray(L0_ffn_mu_k, np.float32)]))
    ident = np.eye(128, dtype=np.float32)

    import jax
    import jax.numpy as jnp
    with jax.default_device(jax.devices("cpu")[0]):
        ivj = 1.0 / jnp.float32(10000.0) ** (
            jnp.arange(0, C, 2, dtype=jnp.float32) / C)
        angj = jnp.arange(N, dtype=jnp.float32)[:, None] * ivj[None, :]
        tab_full = np.ascontiguousarray(np.concatenate(
            [np.asarray(jnp.cos(angj)), np.asarray(jnp.sin(angj))],
            axis=1).astype(np.float32))

    xcm, xtm, tabs, masks = [], [], [], []
    for c in range(NCORES):
        s = c * TLOC
        if c == 0:
            xs = np.concatenate([x[:, 0:TILE, :], x[:, 0:TLOC, :]], axis=1)
            tb = np.concatenate([tab_full[0:TILE], tab_full[0:TLOC]], axis=0)
        else:
            xs = x[:, s - TILE:s + TLOC, :]
            tb = tab_full[s - TILE:s + TLOC]
        xtm.append(np.ascontiguousarray(xs))
        xcm.append(np.ascontiguousarray(xs.transpose(0, 2, 1)))
        tabs.append(np.ascontiguousarray(tb))
        masks.append(np.full((128, 1), 0.0 if c == 0 else 1.0, np.float32))

    nc_a = build_a()
    in_a = [{"xcm": xcm[c], "tab": tabs[c], "wk": wkT, "wv": wvT,
             "bias3": bias3} for c in range(NCORES)]
    res_a = run_bass_kernel_spmd(nc_a, in_a, core_ids=list(range(NCORES)))

    kv_all = np.ascontiguousarray(
        np.stack([res_a.results[c]["kvp"] for c in range(NCORES)]))
    ks_all = np.ascontiguousarray(
        np.stack([res_a.results[c]["ksp"][0] for c in range(NCORES)]))

    nc_b = build_b()
    in_b = [{"xcm": xcm[c], "xtm": xtm[c], "tab": tabs[c], "wq": wqT,
             "wr": wrT, "wkf": wkfT, "wvf": wvfT, "bias3": bias3, "gb": gbv,
             "mus": mus, "mask": masks[c], "ident": ident,
             "kv_all": kv_all, "ks_in": ks_all} for c in range(NCORES)]
    res_b = run_bass_kernel_spmd(nc_b, in_b, core_ids=list(range(NCORES)))

    out = np.concatenate([res_b.results[c]["y"] for c in range(NCORES)],
                         axis=1)
    return out
